# version 05
# baseline (speedup 1.0000x reference)
import numpy as np
import ml_dtypes
import concourse.bass as bass
import concourse.mybir as mybir
from concourse.bass_utils import run_bass_kernel_spmd

F32 = mybir.dt.float32
BF16 = mybir.dt.bfloat16
I32 = mybir.dt.int32
AF = mybir.ActivationFunctionType
OP = mybir.AluOpType
AX = mybir.AxisListType

NCORES = 8
DM = 1024
H = 16
DH = 64
CHK = 64
I3 = 3072
QMAX = 32767.0
EPS = 1e-5
ENGS = ("sync", "scalar", "vector", "tensor", "gpsimd")


class Sched:
    def __init__(self, strict=False):
        self.ops = {e: [] for e in ENGS}
        self.cnt = {s: 0 for s in ('dls', 'dlt', 'dlg', 'dvs', 'dvo', 'dxl', 'dws', 'dxq', 'dch', 'dul', 'dwd', 'dxr', 'dpl', 'dup', 'tt', 'tv', 'tc', 'tg', 'cc')}
        self.strict = strict
        self.last = {e: None for e in ENGS}

    def op(self, eng, fn, waits=(), inc=None, k=1):
        if self.strict and self.last[eng] is not None:
            waits = tuple(waits) + (self.last[eng],)
        if inc is not None:
            self.cnt[inc] += k
            self.last[eng] = (inc, self.cnt[inc])
        self.ops[eng].append((fn, tuple(w for w in waits if w[1] > 0), inc, k))

    def run(self, eng_name, engobj, sems):
        for fn, waits, inc, k in self.ops[eng_name]:
            for s, v in waits:
                engobj.wait_ge(sems[s], v)
            if fn is not None:
                ins = fn(engobj)
                if inc is not None:
                    ins.then_inc(sems[inc], k)


def build_nc(S=8192, ncores=NCORES, dbg=(), strict=False, upto="full"):
    NTB = S // 512
    SBT = min(2048, S)
    NSB = S // SBT
    CW = min(2048, S)
    NCB = S // CW

    nc = bass.Bass("TRN2", target_bir_lowering=False, debug=False,
                   detect_race_conditions=strict)

    x_d = nc.dram_tensor("x", [S, DM], F32, kind="ExternalInput")
    wq_d = nc.dram_tensor("wq", [DM, DM], F32, kind="ExternalInput")
    wk_d = nc.dram_tensor("wk", [DM, DM], F32, kind="ExternalInput")
    wv_d = nc.dram_tensor("wv", [DM, DM], F32, kind="ExternalInput")
    wg_d = nc.dram_tensor("wg", [I3, DM], F32, kind="ExternalInput")
    wu_d = nc.dram_tensor("wu", [I3, DM], F32, kind="ExternalInput")
    wd_d = nc.dram_tensor("wd", [DM, I3], F32, kind="ExternalInput")
    mski_d = nc.dram_tensor("mski", [128, CHK], BF16, kind="ExternalInput")
    msk0_d = nc.dram_tensor("msk0", [128, CHK], BF16, kind="ExternalInput")
    out_d = nc.dram_tensor("out", [S, DM], F32, kind="ExternalOutput")

    x1_sp = nc.dram_tensor("x1_sp", [S, DM], F32)
    xn2t_sp = nc.dram_tensor("xn2t_sp", [DM, S], BF16)
    v_sp = nc.dram_tensor("v_sp", [S, H * (DH + 1)], BF16)
    wgT_sp = nc.dram_tensor("wgT_sp", [24 * 8 * 128, 128], BF16)
    wuT_sp = nc.dram_tensor("wuT_sp", [24 * 8 * 128, 128], BF16)
    wdT_sp = nc.dram_tensor("wdT_sp", [8 * 24 * 128, 128], BF16)
    gt_sp = nc.dram_tensor("gt_sp", [I3, S], BF16)
    up_sp = nc.dram_tensor("up_sp", [I3, S], BF16)
    t_sp = nc.dram_tensor("t_sp", [I3, S], BF16)
    u_sp = nc.dram_tensor("u_sp", [(S // 256) * I3, 256], BF16)
    red_sp = nc.dram_tensor("red_sp", [1, 128], F32)
    sc_sp = nc.dram_tensor("sc_sp", [1, 32], F32)
    ar_in = [nc.dram_tensor(f"ar_in{i}", [1, 16], F32) for i in range(4)]
    ar_out = [nc.dram_tensor(f"ar_out{i}", [1, 16], F32) for i in range(4)]

    dbg_specs = {
        "wgT": (wgT_sp, [24 * 8 * 128, 128], BF16),
        "wuT": (wuT_sp, [24 * 8 * 128, 128], BF16),
        "wdT": (wdT_sp, [8 * 24 * 128, 128], BF16),
        "x1": (x1_sp, [S, DM], F32), "xn2t": (xn2t_sp, [DM, S], BF16),
        "gt": (gt_sp, [I3, S], BF16), "up": (up_sp, [I3, S], BF16),
        "t": (t_sp, [I3, S], BF16), "u": (u_sp, [(S // 256) * I3, 256], BF16),
        "vsp": (v_sp, [S, H * (DH + 1)], BF16),
        "ar0": (ar_out[0], [1, 16], F32), "ar1": (ar_out[1], [1, 16], F32),
        "ar2": (ar_out[2], [1, 16], F32), "ar3": (ar_out[3], [1, 16], F32),
    }
    sbuf_dbg = {}
    for name in dbg:
        if name in ("xn", "att", "ffn"):
            sbuf_dbg[name] = nc.dram_tensor("dbg_" + name, [S, DM], BF16, kind="ExternalOutput")
        elif name in ("qt", "kt", "xq"):
            sbuf_dbg[name] = nc.dram_tensor("dbg_" + name, [DM, S], BF16, kind="ExternalOutput")
        elif name == "expt":
            sbuf_dbg[name] = nc.dram_tensor("dbg_expt", [S * 16, 128], BF16, kind="ExternalOutput")
    dbg_outs = {}
    for name in dbg:
        if name in sbuf_dbg:
            continue
        src, shp, dt = dbg_specs[name]
        dbg_outs[name] = (nc.dram_tensor("dbg_" + name, shp, dt, kind="ExternalOutput"), src)

    sch = Sched(strict=strict)
    C = sch.cnt

    from contextlib import ExitStack
    stack = ExitStack()
    A16 = stack.enter_context(nc.sbuf_tensor("A16", [128, 70656], BF16))
    A32 = stack.enter_context(nc.sbuf_tensor("A32", [128, 8192], F32))
    AI = stack.enter_context(nc.sbuf_tensor("AI", [128, 4096], I32))
    ps = stack.enter_context(nc.psum_tensor("ps", [128, 4096], F32))
    sems = {}
    for sname in ('dls', 'dlt', 'dlg', 'dvs', 'dvo', 'dxl', 'dws', 'dxq', 'dch', 'dul', 'dwd', 'dxr', 'dpl', 'dup', 'tt', 'tv', 'tc', 'tg', 'cc'):
        sems[sname] = stack.enter_context(nc.semaphore(sname))

    def bank(i, cols=512, off=0):
        assert off + cols <= 512
        return ps[:, i * 512 + off: i * 512 + off + cols]

    WQ, WK, WV = 0, 8192, 16384
    MSKI, MSK0 = 24576, 24640
    OB = 24832
    XNT = OB
    QT = OB + 8192
    KT = OB + 16384
    VE = OB + 25600
    VO = OB + 29760
    EXPT = OB + 31840
    ATT = OB + 35936
    XN = OB + 37984
    XN2 = OB + 40032
    XN2T = OB + 42080
    assert OB + 44128 <= 70656
    PWDQ = OB
    PWTS = OB + 6144
    XQ = OB
    WST = OB + 16384
    EVS = OB + 18432
    LD1 = OB + 26624
    SG2 = OB + 30720
    TO3 = OB + 34816
    SCR = OB + 38912
    XQS = OB + 40960
    WDT = OB
    UST = OB + 24576
    OBND = OB + 30720
    FFT = OB + 36864
    FFN_N = OB + 38912
    assert OB + 45056 <= 70656

    a16 = lambda off, n: A16[:, off: off + n]

    XR = 0
    X1B = 4096
    BN6 = 6144
    MVA, MVB = 6156, 6158
    MCOL, VCOL = 6160, 6161
    TMP0, TMP1, TMP2, TMP3, TMP4 = 6162, 6163, 6164, 6165, 6166
    SQC, RSC, NMR, EPSC, NEG8 = 6167, 6168, 6169, 6172, 6176
    RECIP = 6180
    SCAL = 6212
    MXA = 6240
    MXF = 6400
    SRED = 7424
    assert SRED + 512 <= 8192
    a32 = lambda off, n: A32[:, off: off + n]
    col = lambda off: A32[:, off: off + 1]
    ai = lambda off, n: AI[:, off: off + n]

    QSEM = {"sync": "dls", "scalar": "dlt", "gpsimd": "dlg"}

    def dma(q, out, in_, waits=(), sem=None):
        s = sem or QSEM[q]
        sch.op(q, lambda e, out=out, in_=in_: e.dma_start(out=out, in_=in_),
               waits=waits, inc=s, k=16)
        return C[s]

    def sdma(q, sem, out, in_, waits=()):
        return dma(q, out, in_, waits=tuple(waits) + ((sem, C[sem]),), sem=sem)

    def dmat(out, in_, waits=()):
        sch.op("scalar",
               lambda e, out=out, in_=in_: e.dma_start(out=out, in_=in_, transpose=True),
               waits=waits, inc="dlt", k=16)
        return C["dlt"]

    def vop(fn, waits=(), selfw=False):
        if selfw:
            waits = tuple(waits) + (("tv", C["tv"]),)
        sch.op("vector", fn, waits=waits, inc="tv")
        return C["tv"]

    def scop(fn, waits=(), selfw=False):
        if selfw:
            waits = tuple(waits) + (("tc", C["tc"]),)
        sch.op("scalar", fn, waits=waits, inc="tc")
        return C["tc"]

    def top(fn, waits=(), inc=None):
        sch.op("tensor", fn, waits=waits, inc=inc)
        return C["tt"]

    def ts(out, in0, s1, s2, op0, op1=None, accum=None):
        def f(e, out=out, in0=in0, s1=s1, s2=s2, op0=op0, op1=op1, accum=accum):
            kw = {}
            if op1 is not None:
                kw["op1"] = op1
            if accum is not None:
                kw["accum_out"] = accum
            return e.tensor_scalar(out=out, in0=in0, scalar1=s1, scalar2=s2,
                                   op0=op0, **kw)
        return f

    vop(lambda e: e.memset(a16(OB, 70656 - OB), 0.0))
    vop(lambda e: e.memset(a32(0, 8192), 0.0))
    vop(lambda e: e.memset(col(EPSC), EPS))
    CLR = vop(lambda e: e.memset(col(NEG8), -8.0))
    dma("sync", a16(MSKI, CHK), mski_d[:, :])
    MLD = dma("sync", a16(MSK0, CHK), msk0_d[:, :])
    WQKV_LD = 0
    for w_off, w_src in ((WQ, wq_d), (WK, wk_d), (WV, wv_d)):
        for icc in range(8):
            WQKV_LD = dma("gpsimd", a16(w_off + icc * 1024, 1024),
                          w_src[icc * 128:(icc + 1) * 128, :],
                          waits=(("tv", CLR),))

    prev_prep = {}

    ckq = [0]

    def gop(fn, waits=(), selfw=False):
        if selfw:
            waits = tuple(waits) + (("tg", C["tg"]),)
        sch.op("gpsimd", fn, waits=waits, inc="tg")
        return C["tg"]

    def prep_weight(w_src, n_rows, rc, wT_dst):
        nblk = rc // 32
        nic = rc // 128
        for r in range(n_rows):
            buf = r % 2
            f32o = 0 if buf == 0 else 3072
            LD = sdma("sync", "dpl", a32(f32o, rc), w_src[r * 128:(r + 1) * 128, :],
                      waits=(("tv", prev_prep.get(("v", buf), CLR)),))
            wfp = a32(f32o, rc)
            bm = a32(SRED, nblk)
            sbv = a32(SRED + 96 + buf * 96, nblk)
            rb = a32(SRED + 288 + buf * 96, nblk)
            vop(lambda e, wfp=wfp, bm=bm: e.tensor_reduce(
                out=bm.rearrange("p (b o) -> p b o", o=1),
                in_=wfp.rearrange("p (b o) -> p b o", o=32),
                axis=AX.X, op=OP.max, apply_absolute_value=True),
                waits=(("dpl", LD), ("tg", prev_prep.get(("sc", buf), 0))))
            vop(ts(sbv, bm, 1.0 / 7.0, 1e-12, OP.mult, OP.max), selfw=True)
            RBD = vop(lambda e, sbv=sbv, rb=rb: e.reciprocal(rb, sbv), selfw=True)
            wdq = a16(PWDQ + buf * 3072, rc)
            PD = 0
            for ci in range((rc + 1023) // 1024):
                c0 = ci * 1024
                ncb = min(nblk - ci * 32, 32)
                ys = (ckq[0] % 2) * 2048
                ckq[0] += 1
                y = ai(ys, 1024)
                for b in range(ncb):
                    w_ = []
                    if b == 0:
                        w_ = [("tv", RBD), ("tg", prev_prep.get(("y", ys), 0))]
                    vop(ts(y[:, b * 32:(b + 1) * 32],
                           wfp[:, c0 + b * 32: c0 + (b + 1) * 32],
                           rb[:, ci * 32 + b: ci * 32 + b + 1], None, OP.mult),
                        waits=tuple(w_))
                VD = C["tv"]
                for b in range(ncb):
                    w_ = []
                    if b == 0:
                        w_ = [("tv", VD), ("dlt", prev_prep.get(("t", buf), 0))]
                    PD = gop(lambda e, y=y, wdq=wdq, c0=c0, b=b, sbv=sbv, ci=ci:
                             e.tensor_scalar(
                                 out=wdq[:, c0 + b * 32: c0 + (b + 1) * 32],
                                 in0=y[:, b * 32:(b + 1) * 32],
                                 scalar1=sbv[:, ci * 32 + b: ci * 32 + b + 1],
                                 scalar2=None, op0=OP.mult),
                             waits=tuple(w for w in w_ if w[1] > 0))
                prev_prep[("y", ys)] = PD
            prev_prep[("v", buf)] = C["tv"]
            prev_prep[("sc", buf)] = PD
            wts = a16(PWTS + buf * 3072, rc)
            TRD = 0
            for icc in range(nic):
                TRD = dmat(wts[:, icc * 128:(icc + 1) * 128]
                           .rearrange("p (a b) -> p a b", a=1),
                           wdq[:, icc * 128:(icc + 1) * 128],
                           waits=(("tg", PD), ("dls", prev_prep.get(("g", buf), 0))))
            GD = dma("sync",
                     wT_dst[r * nic * 128:(r + 1) * nic * 128, :]
                     .rearrange("(i p) f -> p i f", p=128),
                     wts.rearrange("p (i f) -> p i f", f=128),
                     waits=(("dlt", TRD),))
            prev_prep[("g", buf)] = GD
            prev_prep[("t", buf)] = TRD

    if upto not in ("none",):
        prep_weight(wg_d, 24, DM, wgT_sp)
        prep_weight(wu_d, 24, DM, wuT_sp)
        prep_weight(wd_d, 8, I3, wdT_sp)
    PREP_V = C["tv"]
    PREP_G = C["dlg"]
    PREP_S = C["dls"]
    RUN_ATTN = upto not in ("none", "prep")

    def kt_slot(hp, tbp):
        return KT + hp * 1152 + (64 if tbp == 0 else 640)

    def kt_halo_dst(hp, tbp):
        return KT + hp * 1152 + (576 if tbp == 0 else 0)

    def ln_norm(src_ap, dst_ap, extra_wait=(), dst_waits=()):
        vop(lambda e, src_ap=src_ap: e.bn_stats(out=a32(BN6, 6), in_=src_ap[:, 0:512]),
            waits=extra_wait)
        vop(lambda e, src_ap=src_ap: e.bn_stats(out=a32(BN6 + 6, 6), in_=src_ap[:, 512:1024]))
        vop(lambda e: e.bn_aggr(out=a32(MVA, 2), in_=a32(BN6, 6)), selfw=True)
        vop(lambda e: e.bn_aggr(out=a32(MVB, 2), in_=a32(BN6 + 6, 6)), selfw=True)
        vop(lambda e: e.tensor_tensor(out=col(TMP0), in0=col(MVA), in1=col(MVB), op=OP.add), selfw=True)
        vop(lambda e: e.tensor_tensor(out=col(TMP1), in0=col(MVA), in1=col(MVB), op=OP.subtract), selfw=True)
        vop(lambda e: e.tensor_tensor(out=col(TMP2), in0=col(MVA + 1), in1=col(MVB + 1), op=OP.add), selfw=True)
        vop(lambda e: e.tensor_tensor(out=col(TMP3), in0=col(TMP1), in1=col(TMP1), op=OP.mult), selfw=True)
        vop(ts(col(MCOL), col(TMP0), 0.5, None, OP.mult), selfw=True)
        vop(ts(col(TMP4), col(TMP3), 0.25, None, OP.mult), selfw=True)
        VST = vop(lambda e: e.scalar_tensor_tensor(
            out=col(VCOL), in0=col(TMP2), scalar=0.5, in1=col(TMP4),
            op0=OP.mult, op1=OP.add), selfw=True)
        SCQ = scop(lambda e: e.activation(out=col(SQC), in_=col(VCOL), func=AF.Sqrt,
                                          bias=col(EPSC), scale=1.0),
                   waits=(("tv", VST),))
        vop(lambda e: e.reciprocal(col(RSC), col(SQC)), waits=(("tc", SCQ),))
        VRC = vop(lambda e: e.scalar_tensor_tensor(
            out=col(NMR), in0=col(MCOL), scalar=-1.0, in1=col(RSC),
            op0=OP.mult, op1=OP.mult), selfw=True)
        return scop(lambda e, src_ap=src_ap, dst_ap=dst_ap: e.activation(
            out=dst_ap, in_=src_ap, func=AF.Identity, bias=col(NMR), scale=col(RSC)),
            waits=(("tv", VRC),) + tuple(dst_waits))

    X1D = {}
    X1SP = {}
    VSPW = {}
    DBGD = {}
    PV_of = {}
    DMAT_XN = {}
    T2D_of = {}
    XSPILL = {}
    TBMM = {}
    prev_exp = 0
    prev_attev = (0, 0)
    prev_pj = {}

    for tb in range(NTB if RUN_ATTN else 0):
        tbp = tb % 2
        TRD_tb = 0
        for j in range(4):
            tau = tb * 4 + j
            xs = XR + (tau % 4) * 1024
            XL = sdma("sync", "dxl", a32(xs, 1024), x_d[tau * 128:(tau + 1) * 128, :],
                      waits=(("tv", max(PREP_V, X1D.get(tau - 4, 0))),))
            xn = a16(XN + (tau % 2) * 1024, 1024)
            XND = ln_norm(a32(xs, 1024), xn, extra_wait=(("dxl", XL),),
                          dst_waits=(("dlt", DMAT_XN.get(tau - 2, 0)),
                                     ("dlg", DBGD.get(("xn", tau - 2), 0))))
            sl = XNT + (tau % 8) * 1024
            TRD_tb = dmat(a16(sl, 1024).rearrange("p (c f) -> p c f", f=128),
                          xn, waits=(("tc", XND), ("dlg", PREP_G),
                                     ("tt", TBMM.get(tb - 2, 0))))
            DMAT_XN[tau] = TRD_tb
            if "xn" in sbuf_dbg:
                DBGD[("xn", tau)] = dma("gpsimd",
                    sbuf_dbg["xn"][tau * 128:(tau + 1) * 128, :], xn,
                    waits=(("tc", XND),))
        stage = XNT + (4 * tbp) * 1024

        def rhs_q(i, stage=stage):
            return (a16(stage, 4096).rearrange("p (t f) -> p t f", f=1024)
                    [:, :, i * 128:(i + 1) * 128])

        for which, woff in (("q", WQ), ("k", WK)):
            for oc in range(8):
                bk = oc % 2
                w0 = [("dlt", TRD_tb), ("dlg", WQKV_LD),
                      ("tv", max(prev_pj.get(bk, 0), prev_attev[0])),
                      ("tc", prev_attev[1]),
                      ("dlg", DBGD.get(("pj", tb - 2), 0))]
                MMD = 0
                for i in range(8):
                    MMD = top(lambda e, bk=bk, woff=woff, oc=oc, i=i, stage=stage: e.matmul(
                        bank(bk),
                        a16(woff + i * 1024 + oc * 128, 128),
                        (a16(stage, 4096).rearrange("p (t f) -> p t f", f=1024)
                         [:, :, i * 128:(i + 1) * 128]),
                        start=(i == 0), stop=(i == 7)),
                        waits=(w0 if i == 0 else ()),
                        inc=("tt" if i == 7 else None))
                if which == "q":
                    EV = vop(ts(a16(QT + oc * 1024 + tbp * 512, 512), bank(bk),
                                1.0, None, OP.mult), waits=(("tt", MMD),))
                    if "qt" in sbuf_dbg:
                        DBGD[("pj", tb)] = dma("gpsimd",
                            sbuf_dbg["qt"][oc * 128:(oc + 1) * 128,
                                           tb * 512:(tb + 1) * 512],
                            a16(QT + oc * 1024 + tbp * 512, 512), waits=(("tv", EV),))
                else:
                    vop(ts(a16(kt_slot(oc, tbp), 512), bank(bk), 0.125, None, OP.mult),
                        waits=(("tt", MMD),))
                    EV = vop(ts(a16(kt_halo_dst(oc, tbp), 64), bank(bk, 64, 448),
                                0.125, None, OP.mult))
                    if "kt" in sbuf_dbg:
                        DBGD[("pj", tb)] = dma("gpsimd",
                            sbuf_dbg["kt"][oc * 128:(oc + 1) * 128,
                                           tb * 512:(tb + 1) * 512],
                            a16(kt_slot(oc, tbp), 512), waits=(("tv", EV),))
                prev_pj[bk] = EV
        for j in range(4):
            tau = tb * 4 + j
            for half in range(2):
                bk = half
                w0 = [("tv", max(prev_pj.get(bk, 0), prev_attev[0])),
                      ("tc", prev_attev[1]), ("tt", PV_of.get(tau - 4, 0))]
                if tau - 4 in VSPW:
                    w0.append(("dvs", VSPW[tau - 4]))
                MMD = 0
                for i in range(8):
                    MMD = top(lambda e, bk=bk, i=i, half=half, stage=stage, j=j: e.matmul(
                        bank(bk), a16(stage + j * 1024 + i * 128, 128),
                        a16(WV + i * 1024 + half * 512, 512),
                        start=(i == 0), stop=(i == 7)),
                        waits=(w0 if i == 0 else ()),
                        inc=("tt" if i == 7 else None))
                ve = VE + (tau % 4) * 1040
                vdst = (a16(ve, 1040).rearrange("p (h f) -> p h f", f=65)
                        [:, half * 8:(half + 1) * 8, 0:64])
                EV = vop(ts(vdst, bank(bk), 1.0, None, OP.mult), waits=(("tt", MMD),))
                prev_pj[bk] = EV
            ve = VE + (tau % 4) * 1040
            VED = vop(lambda e, ve=ve: e.memset(
                a16(ve, 1040).rearrange("p (h f) -> p h f", f=65)[:, :, 64:65], 1.0))
            VSPW[tau] = dma("gpsimd", v_sp[tau * 128:(tau + 1) * 128, :], a16(ve, 1040),
                            waits=(("tv", VED), ("dvs", C["dvs"])), sem="dvs")
        QKREADY = C["tv"]
        TBMM[tb] = C["tt"]
        if upto == "proj":
            continue

        for j in range(4):
            tau = tb * 4 + j
            pslot = tau % 2
            base = kt_slot(0, tbp) - KT
            vo = VO + (tau % 2) * 1040
            if tau == 0:
                vodl = sdma("sync", "dvo", A16[64:128, vo:vo + 1040], v_sp[0:64, :],
                            waits=(("dvs", VSPW[tau]), ("tv", CLR)))
            else:
                vodl = sdma("sync", "dvo", a16(vo, 1040),
                            v_sp[tau * 128 - 64: tau * 128 + 64, :],
                            waits=(("dvs", VSPW[tau]), ("tt", PV_of.get(tau - 2, 0))))
            first_qk = True
            QKD = 0
            for mloc, bnkE, bnkO in ((0, 2, 4), (1, 3, 5)):
                wcol = base + j * 128 + mloc * 64 - 64
                for hp in range(8):
                    for par, bnk in ((0, bnkE), (1, bnkO)):
                        w = (("tv", QKREADY), ("tc", prev_exp)) if first_qk else ()
                        first_qk = False
                        QKD = top(lambda e, hp=hp, wcol=wcol, bnk=bnk, j=j,
                                  tbp=tbp, par=par, mloc=mloc: e.matmul(
                            bank(bnk, 64, hp * 64),
                            A16[par * 64: par * 64 + 64,
                                KT + hp * 1152 + wcol: KT + hp * 1152 + wcol + 128],
                            A16[par * 64: par * 64 + 64,
                                QT + hp * 1024 + tbp * 512 + j * 128 + mloc * 64:
                                QT + hp * 1024 + tbp * 512 + j * 128 + mloc * 64 + 64],
                            start=True, stop=True, tile_position=(par * 64, 0)),
                            waits=w, inc=("tt" if hp == 7 else None))
            EXD = 0
            for qi, bnk in enumerate((2, 4, 3, 5)):
                EXD = scop(lambda e, qi=qi, bnk=bnk, pslot=pslot: e.activation(
                    out=a16(EXPT + pslot * 2048 + qi * 512, 512), in_=bank(bnk),
                    func=AF.Exp, bias=col(NEG8), scale=1.0),
                    waits=((("tt", QKD), ("dls", MLD),
                            ("dlg", DBGD.get(("ex", tau - 2), 0))) if qi == 0 else ()))
            prev_exp = EXD
            MKD = 0
            for half in range(2):
                moff = (MSK0 if (tau == 0 and half == 0) else MSKI)
                eap = (a16(EXPT + pslot * 2048 + half * 1024, 1024)
                       .rearrange("p (t f) -> p t f", f=64))
                msk = (a16(moff, 64).rearrange("p (o f) -> p o f", o=1)
                       .broadcast_to((128, 16, 64)))
                MKD = vop(lambda e, eap=eap, msk=msk: e.tensor_tensor(
                    out=eap, in0=eap, in1=msk, op=OP.mult),
                    waits=((("tc", EXD),) if half == 0 else ()))

            def att_slot(h):
                return (6, 65 * h) if h < 7 else ((7, 65 * (h - 7)) if h < 14
                                                  else (0, 65 * (h - 14)))
            if upto == "qkexp":
                prev_attev = (MKD, EXD)
                PV_of[tau] = QKD
                continue
            first_pv = True
            PVD = 0
            for h in range(H):
                hp, par = h // 2, h % 2
                for mloc in range(2):
                    qi = {(0, 0): 0, (0, 1): 1, (1, 0): 2, (1, 1): 3}[(mloc, par)]
                    ecol = EXPT + pslot * 2048 + qi * 512 + hp * 64
                    vtile = (VO + (tau % 2) * 1040) if mloc == 0 else (VE + (tau % 4) * 1040)
                    bnk, soff = att_slot(h)
                    w = ()
                    if first_pv:
                        w = [("tv", MKD), ("tv", prev_attev[0]), ("tc", prev_attev[1]),
                             ("dvo", vodl), ("dlg", DBGD.get(("at", tau - 2), 0))]
                        first_pv = False
                    PVD = top(lambda e, ecol=ecol, vtile=vtile, h=h, bnk=bnk,
                              soff=soff, mloc=mloc: e.matmul(
                        ps[64 * mloc: 64 * mloc + 64,
                           bnk * 512 + soff: bnk * 512 + soff + 65],
                        a16(ecol, 64), a16(vtile + h * 65, 65),
                        start=True, stop=True, tile_position=(0, 64 * mloc)),
                        waits=w, inc=("tt" if (h == H - 1 and mloc == 1) else None))
            PV_of[tau] = PVD
            if upto == "pv":
                prev_attev = (C["tv"], C["tc"])
                continue
            rc = RECIP + pslot * 16
            RCD = 0
            for gi, (bnk, nsl, rbase) in enumerate(((6, 7, 0), (7, 7, 7), (0, 2, 14))):
                RCD = vop(lambda e, bnk=bnk, nsl=nsl, rbase=rbase, rc=rc: e.reciprocal(
                    a32(rc + rbase, nsl),
                    ps[:, bnk * 512 + 64: bnk * 512 + 64 + (nsl - 1) * 65 + 1: 65]),
                    waits=((("tt", PVD),) if gi == 0 else ()))
            attb = ATT + pslot * 1024
            AEV, AEC = 0, 0
            for h in list(range(0, 7)) + [14, 15]:
                bnk, soff = att_slot(h)
                AEV = vop(ts(a16(attb + h * 64, 64),
                             ps[:, bnk * 512 + soff: bnk * 512 + soff + 64],
                             a32(rc + h, 1), None, OP.mult),
                          waits=((("tv", RCD),) if h == 0 else ()))
            for h in range(7, 14):
                bnk, soff = att_slot(h)
                AEC = scop(lambda e, h=h, bnk=bnk, soff=soff, attb=attb, rc=rc: e.activation(
                    out=a16(attb + h * 64, 64),
                    in_=ps[:, bnk * 512 + soff: bnk * 512 + soff + 64],
                    func=AF.Identity, bias=0.0, scale=a32(rc + h, 1)),
                    waits=(("tv", RCD),))
            prev_attev = (AEV, AEC)
            if upto == "evac":
                continue
            if "att" in sbuf_dbg:
                DBGD[("at", tau)] = dma("gpsimd",
                    sbuf_dbg["att"][tau * 128:(tau + 1) * 128, :],
                    a16(attb, 1024), waits=(("tv", AEV), ("tc", AEC)))
            if "expt" in sbuf_dbg:
                DBGD[("ex", tau)] = dma("gpsimd",
                    sbuf_dbg["expt"][tau * 16 * 128: (tau + 1) * 16 * 128, :]
                    .rearrange("(a p) f -> p a f", p=128),
                    a16(EXPT + pslot * 2048, 2048).rearrange("p (a f) -> p a f", f=128),
                    waits=(("tv", MKD),))
            xs = XR + (tau % 4) * 1024
            attn = a16(XN + (tau % 2) * 1024, 1024)
            AND = ln_norm(a16(attb, 1024), attn, extra_wait=(("tc", AEC), ("tv", AEV)),
                          dst_waits=(("dlg", DBGD.get(("xn", tau), 0)),))
            x1b = X1B + (tau % 2) * 1024
            X1V = vop(lambda e, x1b=x1b, xs=xs, attn=attn: e.tensor_tensor(
                out=a32(x1b, 1024), in0=a32(xs, 1024), in1=attn, op=OP.add),
                waits=(("tc", AND), ("dlg", X1SP.get(tau - 2, 0))))
            X1D[tau] = X1V
            X1SP[tau] = dma("gpsimd", x1_sp[tau * 128:(tau + 1) * 128, :], a32(x1b, 1024),
                            waits=(("tv", X1V),))
            xn2 = a16(XN2 + (tau % 2) * 1024, 1024)
            XN2D = ln_norm(a32(x1b, 1024), xn2,
                           dst_waits=(("dlt", T2D_of.get(tau - 2, 0)),))
            scr_a = a16(XN + ((tau + 1) % 2) * 1024, 1024)
            vop(ts(scr_a, xn2, 1.0, None, OP.mult, OP.max, a32(MXA + 2 * (tau % 64), 1)),
                waits=(("tc", XN2D),))
            MXD = vop(ts(scr_a, xn2, -1.0, None, OP.mult, OP.max,
                         a32(MXA + 2 * (tau % 64) + 1, 1)))
            x2t = a16(XN2T + (tau % 2) * 1024, 1024)
            T2D = dmat(x2t.rearrange("p (c f) -> p c f", f=128), xn2,
                       waits=(("tc", XN2D), ("dlg", XSPILL.get(tau - 2, 0))))
            T2D_of[tau] = T2D
            XSPILL[tau] = dma("gpsimd",
                xn2t_sp[:, tau * 128:(tau + 1) * 128].rearrange("(c p) f -> p c f", p=128),
                x2t.rearrange("p (c f) -> p c f", f=128),
                waits=(("dlt", T2D), ("tv", MXD)))
    ATTV, ATTC, ATTG = C["tv"], C["tc"], C["dlg"]
    RUN_AR1 = RUN_ATTN and upto not in ("attn", "proj", "qkexp", "pv", "evac")

    sc_slot = [0]

    def fold_reduce(src_off, nsl, out_off, waits):
        FV = vop(ts(a16(SCR, nsl), a32(src_off, nsl), 1.0, None, OP.mult, OP.max,
                    col(SRED + 256)), waits=waits)
        RG = dma("scalar", red_sp[0:1, :], col(SRED + 256), waits=(("tv", FV),))
        RL = dma("sync", A32[0:1, SRED: SRED + 128], red_sp[0:1, :],
                 waits=(("dlt", RG),))
        return vop(lambda e, out_off=out_off: e.tensor_reduce(
            out=A32[0:1, out_off: out_off + 1], in_=A32[0:1, SRED: SRED + 128],
            axis=AX.X, op=OP.max, apply_absolute_value=False),
            waits=(("dls", RL),), selfw=True)

    def collective(idx, slot_srcs, waits):
        VZ = vop(lambda e: e.memset(A32[0:1, SRED + 304: SRED + 320], 0.0), waits=waits)
        for slot, off in slot_srcs:
            VZ = vop(lambda e, slot=slot, off=off: e.tensor_copy(
                A32[0:1, SRED + 304 + slot: SRED + 305 + slot],
                A32[0:1, off: off + 1]), selfw=True)
        W = dma("scalar", ar_in[idx][:, :], A32[0:1, SRED + 304: SRED + 320],
                waits=(("tv", VZ),))
        sch.op("gpsimd", lambda e, idx=idx: e.collective_compute(
            "AllReduce", OP.max, replica_groups=[list(range(ncores))],
            ins=[ar_in[idx].ap().opt()], outs=[ar_out[idx].ap().opt()]),
            waits=(("dlt", W),), inc="cc", k=1)
        return dma("sync", A32[0:1, SRED + 352: SRED + 368], ar_out[idx][:, :],
                   waits=(("cc", C["cc"]),))

    def bcast(dst_col, src_off, waits):
        slt = sc_slot[0] % 32
        sc_slot[0] += 1
        W = dma("scalar", sc_sp[0:1, slt:slt + 1], A32[0:1, src_off: src_off + 1],
                waits=waits)
        return dma("scalar", col(dst_col), sc_sp[0:1, slt:slt + 1].partition_broadcast(128),
                   waits=(("dlt", W),))

    if not RUN_AR1:
        _dbg_copy(dbg_outs, dma, a16, a32, OB, C, F32)
        sch.op("gpsimd", None, waits=(("dlg", C["dlg"]),))
        return _finish(nc, sch, sems, stack)
    FR = fold_reduce(MXA, 128, SRED + 300,
                     waits=(("tv", ATTV), ("tc", ATTC), ("dlg", ATTG)))
    ARL = collective(0, [(0, SRED + 300)], waits=(("tv", FR),))
    vop(ts(A32[0:1, SRED + 400: SRED + 401], A32[0:1, SRED + 352: SRED + 353],
           1.0 / QMAX, 1e-12, OP.mult, OP.max), waits=(("dls", ARL),))
    SCV = vop(lambda e: e.reciprocal(A32[0:1, SRED + 401: SRED + 402],
                                     A32[0:1, SRED + 400: SRED + 401]), selfw=True)
    bcast(SCAL + 1, SRED + 400, waits=(("tv", SCV),))
    L_rs1 = bcast(SCAL + 0, SRED + 401, waits=(("tv", SCV),))

    prev_f = {}

    def ffn_pass_gen(wT, spill, ready_dls):
        mxn = 0
        prev_sb_mm = C["tt"]
        for sbi in range(NSB):
            XQD = 0
            for dc in range(8):
                buf = dc % 2
                LD = sdma("sync", "dxq", a16(XQS + buf * 2048, SBT),
                          xn2t_sp[dc * 128:(dc + 1) * 128, sbi * SBT:(sbi + 1) * SBT],
                          waits=(("tv", prev_f.get(("ld", buf), 0)), ("dlt", ready_dls),
                                 ("dlg", ATTG), ("tv", ATTV), ("tc", ATTC),
                                 ("dls", PREP_S)))
                vop(ts(ai(0, SBT), a16(XQS + buf * 2048, SBT), col(SCAL + 0), QMAX,
                       OP.mult, OP.min), waits=(("dxq", LD),))
                XQD = vop(ts(a16(XQ + dc * SBT, SBT), ai(0, SBT), 1.0, None, OP.mult),
                          waits=(("tt", prev_sb_mm),
                                 ("dlg", DBGD.get(("xq", dc, sbi - 1), 0))), selfw=True)
                prev_f[("ld", buf)] = XQD
                if "xq" in sbuf_dbg:
                    DBGD[("xq", dc, sbi)] = dma("gpsimd",
                        sbuf_dbg["xq"][dc * 128:(dc + 1) * 128, sbi * SBT:(sbi + 1) * SBT],
                        a16(XQ + dc * SBT, SBT), waits=(("tv", XQD),))
            for oc in range(24):
                wb = oc % 2
                WL = sdma("sync", "dws",
                          a16(WST + wb * 1024, 1024).rearrange("p (i f) -> p i f", f=128),
                          wT[oc * 1024:(oc + 1) * 1024, :].rearrange("(i p) f -> p i f", p=128),
                          waits=(("tt", prev_f.get(("wst", wb), 0)), ("dls", PREP_S)))
                for tb4 in range(SBT // 512):
                    bk = (oc % 2) * 4 + tb4
                    w0 = [("dws", WL), ("tv", max(XQD, prev_f.get(("fbv", bk), 0))),
                          ("tc", prev_f.get(("fbc", bk), 0))]
                    MMD = 0
                    for i in range(8):
                        MMD = top(lambda e, bk=bk, wb=wb, i=i, tb4=tb4: e.matmul(
                            bank(bk), a16(WST + wb * 1024 + i * 128, 128),
                            a16(XQ + i * SBT + tb4 * 512, 512),
                            start=(i == 0), stop=(i == 7)),
                            waits=(w0 if i == 0 else ()),
                            inc=("tt" if i == 7 else None))
                    if tb4 == SBT // 512 - 1:
                        prev_f[("wst", wb)] = MMD
                    prev_sb_mm = MMD
                    ev = EVS + (oc % 4) * 2048 + tb4 * 512
                    ECD = scop(lambda e, ev=ev, bk=bk: e.activation(
                        out=a16(ev, 512), in_=bank(bk), func=AF.Identity, bias=0.0,
                        scale=col(SCAL + 1)),
                        waits=(("tt", MMD), ("dlg", prev_f.get(("sp", oc % 4, tb4), 0))))
                    prev_f[("fbc", bk)] = ECD
                    slot = MXF + (mxn % 768)
                    mxn += 1
                    MXD = vop(lambda e, ev=ev, slot=slot: e.tensor_reduce(
                        out=a32(slot, 1), in_=a16(ev, 512), axis=AX.X, op=OP.max,
                        apply_absolute_value=True), waits=(("tc", ECD),))
                    prev_f[("fbv", bk)] = MXD
                    SPD = dma("gpsimd",
                              spill[oc * 128:(oc + 1) * 128,
                                    sbi * SBT + tb4 * 512: sbi * SBT + (tb4 + 1) * 512],
                              a16(ev, 512), waits=(("tv", MXD),))
                    prev_f[("sp", oc % 4, tb4)] = SPD
                yield
        yield

    def ffn_pass(wT, spill, ready_dls, inter=None):
        g = ffn_pass_gen(wT, spill, ready_dls)
        for _ in g:
            if inter is not None:
                try:
                    next(inter)
                except StopIteration:
                    inter = None
        if inter is not None:
            for _ in inter:
                pass
        return C["tv"], C["dlg"]

    MXCLR = vop(lambda e: e.memset(a32(MXF, 1024), 0.0), waits=(("tv", ATTV),))
    if upto == "ar1":
        sch.op("gpsimd", None, waits=(("dlg", C["dlg"]), ("dls", C["dls"]), ("tv", C["tv"])))
        return _finish(nc, sch, sems, stack)
    GV, GG = ffn_pass(wgT_sp, gt_sp, L_rs1)
    if upto == "gate":
        _dbg_copy(dbg_outs, dma, a16, a32, OB, C, F32)
        sch.op("gpsimd", None, waits=(("dlg", C["dlg"]), ("dls", C["dls"]), ("tv", C["tv"]), ("tc", C["tc"])))
        return _finish(nc, sch, sems, stack)
    FR = fold_reduce(MXF, 768, SRED + 300, waits=(("tv", GV),))
    ARL2 = collective(1, [(0, SRED + 300)], waits=(("tv", FR), ("dlg", GG)))
    vop(ts(A32[0:1, SRED + 404: SRED + 405], A32[0:1, SRED + 352: SRED + 353],
           1.0 / QMAX, 1e-12, OP.mult, OP.max), waits=(("dls", ARL2),))
    vop(lambda e: e.reciprocal(A32[0:1, SRED + 405: SRED + 406],
                               A32[0:1, SRED + 404: SRED + 405]), selfw=True)
    SCV = vop(ts(A32[0:1, SRED + 406: SRED + 407], A32[0:1, SRED + 404: SRED + 405],
                 1.0 / 65536.0, None, OP.mult), selfw=True)
    bcast(SCAL + 2, SRED + 405, waits=(("tv", SCV),))
    bcast(SCAL + 3, SRED + 404, waits=(("tv", SCV),))
    L_s3 = bcast(SCAL + 4, SRED + 406, waits=(("tv", SCV),))
    MXCLR2 = vop(lambda e: e.memset(a32(MXF, 768), 0.0), waits=(("tv", GV),))

    def p1_gen():
        pc = {}
        for ct in range(24 * NCB):
            dc, cb = ct // NCB, ct % NCB
            buf = ct % 2
            gsl = a16(LD1 + buf * 2048, CW)
            LD = sdma("sync", "dch", gsl,
                      gt_sp[dc * 128:(dc + 1) * 128, cb * CW:(cb + 1) * CW],
                      waits=(("tc", pc.get(("y1", buf), 0)), ("dlt", L_s3)))
            Y1D = scop(lambda e, gsl=gsl: e.activation(
                out=ai(2048, CW), in_=gsl, func=AF.Identity, bias=0.0,
                scale=col(SCAL + 2)),
                waits=(("dch", LD), ("tv", pc.get("stt", 0))))
            pc[("y1", buf)] = Y1D
            sg = a16(SG2 + buf * 2048, CW)
            SGD = scop(lambda e, sg=sg: e.activation(
                out=sg, in_=ai(2048, CW), func=AF.Sigmoid, bias=0.0,
                scale=col(SCAL + 3)),
                waits=(("tv", pc.get(("ysq", buf), 0)),), selfw=True)
            YG1F = scop(lambda e: e.activation(
                out=a32(0, CW), in_=ai(2048, CW), func=AF.Identity, bias=0.0,
                scale=col(SCAL + 4)),
                waits=(("tv", pc.get("stt", ATTV)),), selfw=True)
            YSQ = vop(ts(ai(2048, CW), sg, 65536.0, 65535.0, OP.mult, OP.min),
                      waits=(("tc", YG1F),))
            pc[("ysq", buf)] = YSQ
            tt_ = a16(TO3 + buf * 2048, CW)
            vop(lambda e, tt_=tt_: e.scalar_tensor_tensor(
                out=tt_, in0=a32(0, CW), scalar=1.0, in1=ai(2048, CW),
                op0=OP.mult, op1=OP.mult),
                waits=(("dlg", pc.get(("sp", buf), 0)),), selfw=True)
            pc["stt"] = C["tv"]
            TTD = vop(lambda e, tt_=tt_, ct=ct: e.tensor_reduce(
                out=a32(MXF + 768 + ct, 1), in_=tt_, axis=AX.X, op=OP.max,
                apply_absolute_value=True), selfw=True)
            pc[("sp", buf)] = dma("gpsimd",
                t_sp[dc * 128:(dc + 1) * 128, cb * CW:(cb + 1) * CW], tt_,
                waits=(("tv", TTD),))
            yield

    UV, UG = ffn_pass(wuT_sp, up_sp, L_s3, inter=p1_gen())
    if upto == "up":
        _dbg_copy(dbg_outs, dma, a16, a32, OB, C, F32)
        sch.op("gpsimd", None, waits=(("dlg", C["dlg"]), ("dls", C["dls"]), ("tv", C["tv"]), ("tc", C["tc"])))
        return _finish(nc, sch, sems, stack)

    P1V, P1G, P1C = C["tv"], C["dlg"], C["tc"]
    if upto == "p1":
        _dbg_copy(dbg_outs, dma, a16, a32, OB, C, F32)
        sch.op("gpsimd", None, waits=(("dlg", C["dlg"]), ("dls", C["dls"]), ("tv", C["tv"]), ("tc", C["tc"])))
        return _finish(nc, sch, sems, stack)

    FRa = fold_reduce(MXF, 768, SRED + 300, waits=(("tv", max(UV, P1V)),))
    FRb = fold_reduce(MXF + 768, 256, SRED + 301, waits=(("tv", FRa),))
    ARL3 = collective(2, [(0, SRED + 300), (1, SRED + 301)],
                      waits=(("tv", FRb), ("dlg", max(UG, P1G))))
    vop(ts(A32[0:1, SRED + 408: SRED + 409], A32[0:1, SRED + 352: SRED + 353],
           1.0 / QMAX, 1e-12, OP.mult, OP.max), waits=(("dls", ARL3),))
    vop(lambda e: e.reciprocal(A32[0:1, SRED + 409: SRED + 410],
                               A32[0:1, SRED + 408: SRED + 409]), selfw=True)
    vop(ts(A32[0:1, SRED + 410: SRED + 411], A32[0:1, SRED + 353: SRED + 354],
           1.0 / QMAX, 1e-12, OP.mult, OP.max))
    vop(lambda e: e.reciprocal(A32[0:1, SRED + 411: SRED + 412],
                               A32[0:1, SRED + 410: SRED + 411]), selfw=True)
    SCV = vop(lambda e: e.tensor_tensor(
        out=A32[0:1, SRED + 412: SRED + 413], in0=A32[0:1, SRED + 408: SRED + 409],
        in1=A32[0:1, SRED + 410: SRED + 411], op=OP.mult), selfw=True)
    bcast(SCAL + 5, SRED + 409, waits=(("tv", SCV),))
    bcast(SCAL + 6, SRED + 411, waits=(("tv", SCV),))
    L_p2 = bcast(SCAL + 7, SRED + 412, waits=(("tv", SCV),))
    MXCLR3 = vop(lambda e: e.memset(a32(MXF, 1024), 0.0), waits=(("tv", max(UV, P1V)),))

    prev_c = {}
    HW_ = min(1024, CW)
    NH = CW // HW_
    for ct in range(24 * NCB):
        dc, cb = ct // NCB, ct % NCB
        b4 = ct % 4
        tsl = a16(XQ + b4 * 2048, CW)
        usl = a16(WST + b4 * 2048, CW)
        uu = a16(LD1 + b4 * 2048, CW)
        LT = sdma("sync", "dch", tsl,
                  t_sp[dc * 128:(dc + 1) * 128, cb * CW:(cb + 1) * CW],
                  waits=(("tc", prev_c.get(("y2", b4), P1C)), ("dlt", L_p2)))
        LU = sdma("scalar", "dup", usl,
                  up_sp[dc * 128:(dc + 1) * 128, cb * CW:(cb + 1) * CW],
                  waits=(("tc", prev_c.get(("yu", b4), P1C)),
                         ("tv", P1V if ct < 4 else 0),
                         ("dlg", P1G if ct < 4 else 0)))
        Y2D = YUD = 0
        for h in range(NH):
            qa = (ct * NH + h) % 2
            Y2D = scop(lambda e, tsl=tsl, h=h, qa=qa: e.activation(
                out=ai(qa * 1024, HW_), in_=tsl[:, h * HW_:(h + 1) * HW_],
                func=AF.Identity, bias=0.0, scale=col(SCAL + 6)),
                waits=(("dch", LT), prev_c.get(("sttq", qa), ("tv", P1V))))
            YUD = scop(lambda e, usl=usl, h=h, qa=qa: e.activation(
                out=ai(2048 + qa * 1024, HW_), in_=usl[:, h * HW_:(h + 1) * HW_],
                func=AF.Identity, bias=0.0, scale=col(SCAL + 5)),
                waits=(("dup", LU),), selfw=True)
            STT = vop(lambda e, uu=uu, h=h, qa=qa: e.scalar_tensor_tensor(
                out=uu[:, h * HW_:(h + 1) * HW_], in0=ai(qa * 1024, HW_),
                scalar=col(SCAL + 7), in1=ai(2048 + qa * 1024, HW_),
                op0=OP.mult, op1=OP.mult),
                waits=(("tc", YUD), prev_c.get(("sp", b4), ("dlg", P1G))),
                selfw=True)
            prev_c[("sttq", qa)] = ("tv", STT)
            RD = vop(lambda e, uu=uu, h=h, ct=ct: e.tensor_reduce(
                out=a32(MXF + (ct * NH + h) % 768, 1),
                in_=uu[:, h * HW_:(h + 1) * HW_], axis=AX.X, op=OP.max,
                apply_absolute_value=True), selfw=True)
            SPU = 0
            for k8 in range(h * (HW_ // 256), (h + 1) * (HW_ // 256)):
                tbk = cb * (CW // 256) + k8
                SPU = dma("gpsimd",
                          u_sp[tbk * I3 + dc * 128: tbk * I3 + (dc + 1) * 128, :],
                          uu[:, k8 * 256:(k8 + 1) * 256], waits=(("tv", RD),))
        prev_c[("y2", b4)] = Y2D
        prev_c[("yu", b4)] = YUD
        prev_c[("sp", b4)] = ("dlg", SPU)
    P2V, P2G, P2S = C["tv"], C["dlg"], C["dls"]
    if upto == "p2":
        _dbg_copy(dbg_outs, dma, a16, a32, OB, C, F32)
        sch.op("gpsimd", None, waits=(("dlg", C["dlg"]), ("dls", C["dls"]), ("tv", C["tv"]), ("tc", C["tc"])))
        return _finish(nc, sch, sems, stack)

    FR = fold_reduce(MXF, 768, SRED + 300, waits=(("tv", P2V),))
    ARL4 = collective(3, [(0, SRED + 300)],
                      waits=(("tv", FR), ("dlg", P2G), ("dls", P2S)))
    vop(ts(A32[0:1, SRED + 414: SRED + 415], A32[0:1, SRED + 352: SRED + 353],
           1.0 / QMAX, 1e-12, OP.mult, OP.max), waits=(("dls", ARL4),))
    SCV = vop(lambda e: e.reciprocal(A32[0:1, SRED + 415: SRED + 416],
                                     A32[0:1, SRED + 414: SRED + 415]), selfw=True)
    bcast(SCAL + 8, SRED + 415, waits=(("tv", SCV),))
    L_p8 = bcast(SCAL + 9, SRED + 414, waits=(("tv", SCV),))

    WDL = 0
    wdT_all = wdT_sp[:, :].rearrange("(jc icc p) f -> p jc icc f", p=128, icc=24)
    for icc in range(24):
        WDL = sdma("sync", "dwd",
                   a16(WDT + icc * 1024, 1024).rearrange("p (i f) -> p i f", f=128),
                   wdT_all[:, :, icc, :],
                   waits=(("tv", P2V), ("tc", C["tc"]), ("dls", PREP_S)))
    prev_p8 = {}
    NBLK = S // 256
    ODMA = 0
    TTF = C["tt"]
    uinfo = {}
    for b in range(NBLK + 1):
        if b < NBLK:
            us = 12288 + (b % 2) * 6144
            ob2 = (b % 2) * 6144
            ULD = sdma("sync", "dul",
                       a16(us, 6144).rearrange("p (i f) -> p i f", f=256),
                       u_sp[b * I3: (b + 1) * I3, :].rearrange("(i p) f -> p i f", p=128),
                       waits=(("tc", prev_p8.get(("ust", b % 2), 0)), ("dlt", L_p8)))
            OBD = 0
            for hh in range(2):
                YOD = scop(lambda e, hh=hh, us=us: e.activation(
                    out=ai(0, 3072), in_=a16(us + hh * 3072, 3072), func=AF.Identity,
                    bias=0.0, scale=col(SCAL + 8)),
                    waits=((("dul", ULD), ("tv", P2V if b == 0 else 0))
                           if hh == 0 else ()),
                    selfw=True)
                prev_p8[("ust", b % 2)] = YOD
                OBD = scop(lambda e, hh=hh, ob2=ob2: e.activation(
                    out=a16(ob2 + hh * 3072, 3072), in_=ai(0, 3072), func=AF.Identity,
                    bias=0.0, scale=1.0),
                    waits=((("tt", prev_p8.get(("mmb", b % 2), TTF)),)
                           if hh == 0 else ()),
                    selfw=True)
            uinfo[b] = (OBD, ob2)
        if b == 0:
            continue
        bb = b - 1
        OBDp, obp = uinfo.pop(bb)
        grp = (bb % 2) * 4
        MMD = 0
        for ti in range(2):
            tau = bb * 2 + ti
            for jh in range(2):
                bk = grp + ti * 2 + jh
                w0 = [("tc", OBDp), ("dwd", WDL),
                      ("tv", prev_p8.get(("ev", bk), 0)), ("tc", prev_p8.get(("ec", bk), 0))]
                MMD = 0
                for icc in range(24):
                    MMD = top(lambda e, bk=bk, icc=icc, ti=ti, jh=jh, obp=obp: e.matmul(
                        bank(bk),
                        a16(obp, 6144).rearrange("p (i f) -> p i f", f=256)
                        [:, icc, ti * 128:(ti + 1) * 128],
                        a16(WDT + icc * 1024 + jh * 512, 512),
                        start=(icc == 0), stop=(icc == 23)),
                        waits=(w0 if icc == 0 else ()),
                        inc=("tt" if icc == 23 else None))
                ECD = scop(lambda e, bk=bk, ti=ti, jh=jh: e.activation(
                    out=a16(FFT + ti * 1024 + jh * 512, 512), in_=bank(bk),
                    func=AF.Identity, bias=0.0, scale=col(SCAL + 9)),
                    waits=(("tt", MMD), ("dlg", DBGD.get(("ff", tau - 2), 0))))
                prev_p8[("ec", bk)] = ECD
            fft = a16(FFT + ti * 1024, 1024)
            if "ffn" in sbuf_dbg:
                DBGD[("ff", tau)] = dma("gpsimd",
                    sbuf_dbg["ffn"][tau * 128:(tau + 1) * 128, :], fft,
                    waits=(("tc", ECD),))
            ffn_n = a16(FFN_N + ti * 1024, 1024)
            x1r = XR + (tau % 2) * 1024
            XLD = sdma("sync", "dxr", a32(x1r, 1024), x1_sp[tau * 128:(tau + 1) * 128, :],
                       waits=(("tv", prev_p8.get(("x1", tau % 2), P1V)),))
            FND = ln_norm(fft, ffn_n, extra_wait=(("tc", ECD),))
            ob_ = 2048 + (tau % 2) * 1024
            OV = vop(lambda e, ob_=ob_, x1r=x1r, ffn_n=ffn_n: e.tensor_tensor(
                out=a32(ob_, 1024), in0=a32(x1r, 1024), in1=ffn_n, op=OP.add),
                waits=(("tc", FND), ("dxr", XLD),
                       ("dlg", prev_p8.get(("od", tau % 2), 0))))
            prev_p8[("x1", tau % 2)] = OV
            for jh in range(2):
                prev_p8[("ev", grp + ti * 2 + jh)] = OV
            ODMA = dma("gpsimd", out_d[tau * 128:(tau + 1) * 128, :], a32(ob_, 1024),
                       waits=(("tv", OV),))
            prev_p8[("od", tau % 2)] = ODMA
        prev_p8[("mmb", bb % 2)] = MMD

    _dbg_copy(dbg_outs, dma, a16, a32, OB, C, F32)
    sch.op("gpsimd", None, waits=(("dlg", C["dlg"]),))
    return _finish(nc, sch, sems, stack)


def _dbg_copy(dbg_outs, dma, a16, a32, OB, C, F32):
    first = True
    ping = [0]
    for name, (dst, srct) in dbg_outs.items():
        rows, cols = dst.ap().shape
        isf32 = dst.ap().dtype == F32
        cmax = 2048 if isf32 else 8192
        for r0 in range(0, rows, 128):
            rr = min(128, rows - r0)
            for c0 in range(0, cols, cmax):
                cc = min(cmax, cols - c0)
                if isf32:
                    slot = (ping[0] % 2) * 2048
                    stg_full = a32(slot, cc)
                else:
                    slot = OB + (ping[0] % 2) * 8192
                    stg_full = a16(slot, cc)
                ping[0] += 1
                stg = stg_full if rr == 128 else stg_full[0:rr, :]
                w = [("dls", C["dls"]), ("dlg", C["dlg"])]
                if first:
                    w += [("tv", C["tv"]), ("tc", C["tc"])]
                    first = False
                LD = dma("sync", stg, srct[r0:r0 + rr, c0:c0 + cc], waits=tuple(w))
                dma("gpsimd", dst[r0:r0 + rr, c0:c0 + cc], stg, waits=(("dls", LD),))


def _finish(nc, sch, sems, stack):
    with nc.Block() as block:
        @block.sync
        def _(sync):
            sch.run("sync", sync, sems)

        @block.scalar
        def _(scalar):
            sch.run("scalar", scalar, sems)

        @block.vector
        def _(vector):
            sch.run("vector", vector, sems)

        @block.tensor
        def _(tensor):
            sch.run("tensor", tensor, sems)

        @block.gpsimd
        def _(gpsimd):
            sch.run("gpsimd", gpsimd, sems)

    stack.close()
    return nc


def make_masks():
    tri = (np.arange(CHK)[:, None] <= np.arange(CHK)[None, :]).astype(np.float32)
    mski = np.ones((128, CHK), np.float32)
    mski[64:128, :] = tri
    msk0 = np.zeros((128, CHK), np.float32)
    msk0[64:128, :] = tri
    return mski.astype(ml_dtypes.bfloat16), msk0.astype(ml_dtypes.bfloat16)


def kernel(L0_attn_hidden_states, L0_attn_Wq, L0_attn_Wk, L0_attn_Wv,
           L0_ffn_w_gate, L0_ffn_w_up, L0_ffn_w_down):
    x = np.ascontiguousarray(np.asarray(L0_attn_hidden_states, dtype=np.float32))
    B, S_, D_ = x.shape
    assert B == NCORES and D_ == DM
    mski, msk0 = make_masks()
    common = dict(
        wq=np.ascontiguousarray(np.asarray(L0_attn_Wq, np.float32)),
        wk=np.ascontiguousarray(np.asarray(L0_attn_Wk, np.float32)),
        wv=np.ascontiguousarray(np.asarray(L0_attn_Wv, np.float32)),
        wg=np.ascontiguousarray(np.asarray(L0_ffn_w_gate, np.float32)),
        wu=np.ascontiguousarray(np.asarray(L0_ffn_w_up, np.float32)),
        wd=np.ascontiguousarray(np.asarray(L0_ffn_w_down, np.float32)),
        mski=mski, msk0=msk0,
    )
    nc = build_nc(S=S_, ncores=NCORES)
    in_maps = [dict(common, x=np.ascontiguousarray(x[c])) for c in range(NCORES)]
    res = run_bass_kernel_spmd(nc, in_maps, core_ids=list(range(NCORES)))
    out = np.stack([np.asarray(res.results[c]["out"]) for c in range(NCORES)], axis=0)
    return out.astype(np.float32)


# version 06
# speedup vs baseline: 1.2783x; 1.2783x over previous
import numpy as np
import ml_dtypes
import concourse.bass as bass
import concourse.mybir as mybir
from concourse.bass_utils import run_bass_kernel_spmd

F32 = mybir.dt.float32
BF16 = mybir.dt.bfloat16
I32 = mybir.dt.int32
AF = mybir.ActivationFunctionType
OP = mybir.AluOpType
AX = mybir.AxisListType

NCORES = 8
DM = 1024
H = 16
DH = 64
CHK = 64
I3 = 3072
QMAX = 32767.0
EPS = 1e-5
ENGS = ("sync", "scalar", "vector", "tensor", "gpsimd")


class Sched:
    def __init__(self, strict=False):
        self.ops = {e: [] for e in ENGS}
        self.cnt = {s: 0 for s in ('dls', 'dlt', 'dlg', 'dvs', 'dvo', 'dxl', 'dws', 'dxq', 'dch', 'dul', 'dwd', 'dxr', 'dpl', 'dup', 'tt', 'tv', 'tc', 'tg', 'cc')}
        self.strict = strict
        self.last = {e: None for e in ENGS}

    def op(self, eng, fn, waits=(), inc=None, k=1):
        if self.strict and self.last[eng] is not None:
            waits = tuple(waits) + (self.last[eng],)
        if inc is not None:
            self.cnt[inc] += k
            self.last[eng] = (inc, self.cnt[inc])
        self.ops[eng].append((fn, tuple(w for w in waits if w[1] > 0), inc, k))

    def run(self, eng_name, engobj, sems):
        for fn, waits, inc, k in self.ops[eng_name]:
            for s, v in waits:
                engobj.wait_ge(sems[s], v)
            if fn is not None:
                ins = fn(engobj)
                if inc is not None:
                    ins.then_inc(sems[inc], k)


def build_nc(S=8192, ncores=NCORES, dbg=(), strict=False, upto="full"):
    NTB = S // 512
    SBT = min(2048, S)
    NSB = S // SBT
    CW = min(2048, S)
    NCB = S // CW

    nc = bass.Bass("TRN2", target_bir_lowering=False, debug=False,
                   detect_race_conditions=strict)

    x_d = nc.dram_tensor("x", [S, DM], F32, kind="ExternalInput")
    wq_d = nc.dram_tensor("wq", [DM, DM], F32, kind="ExternalInput")
    wk_d = nc.dram_tensor("wk", [DM, DM], F32, kind="ExternalInput")
    wv_d = nc.dram_tensor("wv", [DM, DM], F32, kind="ExternalInput")
    wg_d = nc.dram_tensor("wg", [I3, DM], F32, kind="ExternalInput")
    wu_d = nc.dram_tensor("wu", [I3, DM], F32, kind="ExternalInput")
    wd_d = nc.dram_tensor("wd", [DM, I3], F32, kind="ExternalInput")
    mski_d = nc.dram_tensor("mski", [128, CHK], BF16, kind="ExternalInput")
    msk0_d = nc.dram_tensor("msk0", [128, CHK], BF16, kind="ExternalInput")
    out_d = nc.dram_tensor("out", [S, DM], F32, kind="ExternalOutput")

    x1_sp = nc.dram_tensor("x1_sp", [S, DM], F32)
    xn2t_sp = nc.dram_tensor("xn2t_sp", [DM, S], BF16)
    v_sp = nc.dram_tensor("v_sp", [S, H * (DH + 1)], BF16)
    wgT_sp = nc.dram_tensor("wgT_sp", [24 * 8 * 128, 128], BF16)
    wuT_sp = nc.dram_tensor("wuT_sp", [24 * 8 * 128, 128], BF16)
    wdT_sp = nc.dram_tensor("wdT_sp", [8 * 24 * 128, 128], BF16)
    gt_sp = nc.dram_tensor("gt_sp", [I3, S], BF16)
    up_sp = nc.dram_tensor("up_sp", [I3, S], BF16)
    t_sp = nc.dram_tensor("t_sp", [I3, S], BF16)
    u_sp = nc.dram_tensor("u_sp", [(S // 256) * I3, 256], BF16)
    red_sp = nc.dram_tensor("red_sp", [1, 128], F32)
    sc_sp = nc.dram_tensor("sc_sp", [1, 32], F32)
    ar_in = [nc.dram_tensor(f"ar_in{i}", [1, 16], F32) for i in range(4)]
    ar_out = [nc.dram_tensor(f"ar_out{i}", [1, 16], F32) for i in range(4)]

    dbg_specs = {
        "wgT": (wgT_sp, [24 * 8 * 128, 128], BF16),
        "wuT": (wuT_sp, [24 * 8 * 128, 128], BF16),
        "wdT": (wdT_sp, [8 * 24 * 128, 128], BF16),
        "x1": (x1_sp, [S, DM], F32), "xn2t": (xn2t_sp, [DM, S], BF16),
        "gt": (gt_sp, [I3, S], BF16), "up": (up_sp, [I3, S], BF16),
        "t": (t_sp, [I3, S], BF16), "u": (u_sp, [(S // 256) * I3, 256], BF16),
        "vsp": (v_sp, [S, H * (DH + 1)], BF16),
        "ar0": (ar_out[0], [1, 16], F32), "ar1": (ar_out[1], [1, 16], F32),
        "ar2": (ar_out[2], [1, 16], F32), "ar3": (ar_out[3], [1, 16], F32),
    }
    sbuf_dbg = {}
    for name in dbg:
        if name in ("xn", "att", "ffn"):
            sbuf_dbg[name] = nc.dram_tensor("dbg_" + name, [S, DM], BF16, kind="ExternalOutput")
        elif name in ("qt", "kt", "xq"):
            sbuf_dbg[name] = nc.dram_tensor("dbg_" + name, [DM, S], BF16, kind="ExternalOutput")
        elif name == "expt":
            sbuf_dbg[name] = nc.dram_tensor("dbg_expt", [S * 16, 128], BF16, kind="ExternalOutput")
    dbg_outs = {}
    for name in dbg:
        if name in sbuf_dbg:
            continue
        src, shp, dt = dbg_specs[name]
        dbg_outs[name] = (nc.dram_tensor("dbg_" + name, shp, dt, kind="ExternalOutput"), src)

    sch = Sched(strict=strict)
    C = sch.cnt

    from contextlib import ExitStack
    stack = ExitStack()
    A16 = stack.enter_context(nc.sbuf_tensor("A16", [128, 70656], BF16))
    A32 = stack.enter_context(nc.sbuf_tensor("A32", [128, 8192], F32))
    AI = stack.enter_context(nc.sbuf_tensor("AI", [128, 4096], I32))
    ps = stack.enter_context(nc.psum_tensor("ps", [128, 4096], F32))
    sems = {}
    for sname in ('dls', 'dlt', 'dlg', 'dvs', 'dvo', 'dxl', 'dws', 'dxq', 'dch', 'dul', 'dwd', 'dxr', 'dpl', 'dup', 'tt', 'tv', 'tc', 'tg', 'cc'):
        sems[sname] = stack.enter_context(nc.semaphore(sname))

    def bank(i, cols=512, off=0):
        assert off + cols <= 512
        return ps[:, i * 512 + off: i * 512 + off + cols]

    WQ, WK, WV = 0, 8192, 16384
    MSKI, MSK0 = 24576, 24640
    OB = 24832
    XNT = OB
    QT = OB + 8192
    KT = OB + 16384
    VE = OB + 25600
    VO = OB + 29760
    EXPT = OB + 31840
    ATT = OB + 35936
    XN = OB + 37984
    XN2 = OB + 40032
    XN2T = OB + 42080
    assert OB + 44128 <= 70656
    PWDQ = OB
    PWTS = OB + 6144
    XQ = OB
    WST = OB + 16384
    EVS = OB + 18432
    LD1 = OB + 26624
    SG2 = OB + 30720
    TO3 = OB + 34816
    SCR = OB + 38912
    XQS = OB + 40960
    WDT = OB
    UST = OB + 24576
    OBND = OB + 30720
    FFT = OB + 36864
    FFN_N = OB + 38912
    assert OB + 45056 <= 70656

    a16 = lambda off, n: A16[:, off: off + n]

    XR = 0
    X1B = 4096
    BN6 = 6144
    MVA, MVB = 6156, 6158
    MCOL, VCOL = 6160, 6161
    TMP0, TMP1, TMP2, TMP3, TMP4 = 6162, 6163, 6164, 6165, 6166
    SQC, RSC, NMR, EPSC, NEG8 = 6167, 6168, 6169, 6172, 6176
    RECIP = 6180
    SCAL = 6212
    MXA = 6240
    MXF = 6400
    SRED = 7424
    assert SRED + 512 <= 8192
    a32 = lambda off, n: A32[:, off: off + n]
    col = lambda off: A32[:, off: off + 1]
    ai = lambda off, n: AI[:, off: off + n]

    QSEM = {"sync": "dls", "scalar": "dlt", "gpsimd": "dlg"}

    def dma(q, out, in_, waits=(), sem=None):
        s = sem or QSEM[q]
        sch.op(q, lambda e, out=out, in_=in_: e.dma_start(out=out, in_=in_),
               waits=waits, inc=s, k=16)
        return C[s]

    def sdma(q, sem, out, in_, waits=()):
        return dma(q, out, in_, waits=tuple(waits) + ((sem, C[sem]),), sem=sem)

    def dmat(out, in_, waits=()):
        sch.op("scalar",
               lambda e, out=out, in_=in_: e.dma_start(out=out, in_=in_, transpose=True),
               waits=waits, inc="dlt", k=16)
        return C["dlt"]

    def vop(fn, waits=(), selfw=False):
        if selfw:
            waits = tuple(waits) + (("tv", C["tv"]),)
        sch.op("vector", fn, waits=waits, inc="tv")
        return C["tv"]

    def scop(fn, waits=(), selfw=False):
        if selfw:
            waits = tuple(waits) + (("tc", C["tc"]),)
        sch.op("scalar", fn, waits=waits, inc="tc")
        return C["tc"]

    def top(fn, waits=(), inc=None):
        sch.op("tensor", fn, waits=waits, inc=inc)
        return C["tt"]

    def ts(out, in0, s1, s2, op0, op1=None, accum=None):
        def f(e, out=out, in0=in0, s1=s1, s2=s2, op0=op0, op1=op1, accum=accum):
            kw = {}
            if op1 is not None:
                kw["op1"] = op1
            if accum is not None:
                kw["accum_out"] = accum
            return e.tensor_scalar(out=out, in0=in0, scalar1=s1, scalar2=s2,
                                   op0=op0, **kw)
        return f

    vop(lambda e: e.memset(a16(OB, 70656 - OB), 0.0))
    vop(lambda e: e.memset(a32(0, 8192), 0.0))
    vop(lambda e: e.memset(col(EPSC), EPS))
    CLR = vop(lambda e: e.memset(col(NEG8), -8.0))
    dma("sync", a16(MSKI, CHK), mski_d[:, :])
    MLD = dma("sync", a16(MSK0, CHK), msk0_d[:, :])
    WQKV_LD = 0
    for w_off, w_src in ((WQ, wq_d), (WK, wk_d), (WV, wv_d)):
        for icc in range(8):
            WQKV_LD = dma("gpsimd", a16(w_off + icc * 1024, 1024),
                          w_src[icc * 128:(icc + 1) * 128, :],
                          waits=(("tv", CLR),))

    prev_prep = {}

    def prep_weight(w_src, n_rows, rc, wT_dst):
        nblk = rc // 32
        nic = rc // 128
        for r in range(n_rows):
            buf = r % 2
            f32o = 0 if buf == 0 else 3072
            LD = sdma("sync", "dpl", a32(f32o, rc), w_src[r * 128:(r + 1) * 128, :],
                      waits=(("tv", prev_prep.get(("v", buf), CLR)),))
            wfp = a32(f32o, rc)
            bm, sbv, rb = a32(SRED, nblk), a32(SRED + 96, nblk), a32(SRED + 192, nblk)
            vop(lambda e, wfp=wfp, bm=bm: e.tensor_reduce(
                out=bm.rearrange("p (b o) -> p b o", o=1),
                in_=wfp.rearrange("p (b o) -> p b o", o=32),
                axis=AX.X, op=OP.max, apply_absolute_value=True),
                waits=(("dpl", LD),))
            vop(ts(sbv, bm, 1.0 / 7.0, 1e-12, OP.mult, OP.max), selfw=True)
            RBD = vop(lambda e, sbv=sbv, rb=rb: e.reciprocal(rb, sbv), selfw=True)
            y = ai(0, rc)
            wdq = a16(PWDQ + buf * 3072, rc)
            for b in range(nblk):
                vop(ts(y[:, b * 32:(b + 1) * 32], wfp[:, b * 32:(b + 1) * 32],
                       rb[:, b:b + 1], None, OP.mult),
                    waits=((("tv", RBD),) if b == 0 else ()))
            for b in range(nblk):
                vop(ts(wdq[:, b * 32:(b + 1) * 32], y[:, b * 32:(b + 1) * 32],
                       sbv[:, b:b + 1], None, OP.mult),
                    waits=((("dlt", prev_prep.get(("t", buf), 0)), ("tv", RBD)) if b == 0 else ()))
            VD = C["tv"]
            prev_prep[("v", buf)] = VD
            wts = a16(PWTS + buf * 3072, rc)
            TRD = 0
            for icc in range(nic):
                TRD = dmat(wts[:, icc * 128:(icc + 1) * 128]
                           .rearrange("p (a b) -> p a b", a=1),
                           wdq[:, icc * 128:(icc + 1) * 128],
                           waits=(("tv", VD), ("dlg", prev_prep.get(("g", buf), 0))))
            GD = dma("gpsimd",
                     wT_dst[r * nic * 128:(r + 1) * nic * 128, :]
                     .rearrange("(i p) f -> p i f", p=128),
                     wts.rearrange("p (i f) -> p i f", f=128),
                     waits=(("dlt", TRD),))
            prev_prep[("g", buf)] = GD
            prev_prep[("t", buf)] = TRD

    if upto not in ("none",):
        prep_weight(wg_d, 24, DM, wgT_sp)
        prep_weight(wu_d, 24, DM, wuT_sp)
        prep_weight(wd_d, 8, I3, wdT_sp)
    PREP_V = C["tv"]
    PREP_G = C["dlg"]
    RUN_ATTN = upto not in ("none", "prep")

    def kt_slot(hp, tbp):
        return KT + hp * 1152 + (64 if tbp == 0 else 640)

    def kt_halo_dst(hp, tbp):
        return KT + hp * 1152 + (576 if tbp == 0 else 0)

    def ln_norm(src_ap, dst_ap, extra_wait=(), dst_waits=()):
        vop(lambda e, src_ap=src_ap: e.bn_stats(out=a32(BN6, 6), in_=src_ap[:, 0:512]),
            waits=extra_wait)
        vop(lambda e, src_ap=src_ap: e.bn_stats(out=a32(BN6 + 6, 6), in_=src_ap[:, 512:1024]))
        vop(lambda e: e.bn_aggr(out=a32(MVA, 2), in_=a32(BN6, 6)), selfw=True)
        vop(lambda e: e.bn_aggr(out=a32(MVB, 2), in_=a32(BN6 + 6, 6)), selfw=True)
        vop(lambda e: e.tensor_tensor(out=col(TMP0), in0=col(MVA), in1=col(MVB), op=OP.add), selfw=True)
        vop(lambda e: e.tensor_tensor(out=col(TMP1), in0=col(MVA), in1=col(MVB), op=OP.subtract), selfw=True)
        vop(lambda e: e.tensor_tensor(out=col(TMP2), in0=col(MVA + 1), in1=col(MVB + 1), op=OP.add), selfw=True)
        vop(lambda e: e.tensor_tensor(out=col(TMP3), in0=col(TMP1), in1=col(TMP1), op=OP.mult), selfw=True)
        vop(ts(col(MCOL), col(TMP0), 0.5, None, OP.mult), selfw=True)
        vop(ts(col(TMP4), col(TMP3), 0.25, None, OP.mult), selfw=True)
        VST = vop(lambda e: e.scalar_tensor_tensor(
            out=col(VCOL), in0=col(TMP2), scalar=0.5, in1=col(TMP4),
            op0=OP.mult, op1=OP.add), selfw=True)
        SCQ = scop(lambda e: e.activation(out=col(SQC), in_=col(VCOL), func=AF.Sqrt,
                                          bias=col(EPSC), scale=1.0),
                   waits=(("tv", VST),))
        vop(lambda e: e.reciprocal(col(RSC), col(SQC)), waits=(("tc", SCQ),))
        VRC = vop(lambda e: e.scalar_tensor_tensor(
            out=col(NMR), in0=col(MCOL), scalar=-1.0, in1=col(RSC),
            op0=OP.mult, op1=OP.mult), selfw=True)
        return scop(lambda e, src_ap=src_ap, dst_ap=dst_ap: e.activation(
            out=dst_ap, in_=src_ap, func=AF.Identity, bias=col(NMR), scale=col(RSC)),
            waits=(("tv", VRC),) + tuple(dst_waits))

    X1D = {}
    X1SP = {}
    VSPW = {}
    DBGD = {}
    PV_of = {}
    DMAT_XN = {}
    T2D_of = {}
    XSPILL = {}
    TBMM = {}
    prev_exp = 0
    prev_attev = (0, 0)
    prev_pj = {}

    for tb in range(NTB if RUN_ATTN else 0):
        tbp = tb % 2
        TRD_tb = 0
        for j in range(4):
            tau = tb * 4 + j
            xs = XR + (tau % 4) * 1024
            XL = sdma("sync", "dxl", a32(xs, 1024), x_d[tau * 128:(tau + 1) * 128, :],
                      waits=(("tv", max(PREP_V, X1D.get(tau - 4, 0))),))
            xn = a16(XN + (tau % 2) * 1024, 1024)
            XND = ln_norm(a32(xs, 1024), xn, extra_wait=(("dxl", XL),),
                          dst_waits=(("dlt", DMAT_XN.get(tau - 2, 0)),
                                     ("dlg", DBGD.get(("xn", tau - 2), 0))))
            sl = XNT + (tau % 8) * 1024
            TRD_tb = dmat(a16(sl, 1024).rearrange("p (c f) -> p c f", f=128),
                          xn, waits=(("tc", XND), ("dlg", PREP_G),
                                     ("tt", TBMM.get(tb - 2, 0))))
            DMAT_XN[tau] = TRD_tb
            if "xn" in sbuf_dbg:
                DBGD[("xn", tau)] = dma("gpsimd",
                    sbuf_dbg["xn"][tau * 128:(tau + 1) * 128, :], xn,
                    waits=(("tc", XND),))
        stage = XNT + (4 * tbp) * 1024

        def rhs_q(i, stage=stage):
            return (a16(stage, 4096).rearrange("p (t f) -> p t f", f=1024)
                    [:, :, i * 128:(i + 1) * 128])

        for which, woff in (("q", WQ), ("k", WK)):
            for oc in range(8):
                bk = oc % 2
                w0 = [("dlt", TRD_tb), ("dlg", WQKV_LD),
                      ("tv", max(prev_pj.get(bk, 0), prev_attev[0])),
                      ("tc", prev_attev[1]),
                      ("dlg", DBGD.get(("pj", tb - 2), 0))]
                MMD = 0
                for i in range(8):
                    MMD = top(lambda e, bk=bk, woff=woff, oc=oc, i=i, stage=stage: e.matmul(
                        bank(bk),
                        a16(woff + i * 1024 + oc * 128, 128),
                        (a16(stage, 4096).rearrange("p (t f) -> p t f", f=1024)
                         [:, :, i * 128:(i + 1) * 128]),
                        start=(i == 0), stop=(i == 7)),
                        waits=(w0 if i == 0 else ()),
                        inc=("tt" if i == 7 else None))
                if which == "q":
                    EV = vop(ts(a16(QT + oc * 1024 + tbp * 512, 512), bank(bk),
                                1.0, None, OP.mult), waits=(("tt", MMD),))
                    if "qt" in sbuf_dbg:
                        DBGD[("pj", tb)] = dma("gpsimd",
                            sbuf_dbg["qt"][oc * 128:(oc + 1) * 128,
                                           tb * 512:(tb + 1) * 512],
                            a16(QT + oc * 1024 + tbp * 512, 512), waits=(("tv", EV),))
                else:
                    vop(ts(a16(kt_slot(oc, tbp), 512), bank(bk), 0.125, None, OP.mult),
                        waits=(("tt", MMD),))
                    EV = vop(ts(a16(kt_halo_dst(oc, tbp), 64), bank(bk, 64, 448),
                                0.125, None, OP.mult))
                    if "kt" in sbuf_dbg:
                        DBGD[("pj", tb)] = dma("gpsimd",
                            sbuf_dbg["kt"][oc * 128:(oc + 1) * 128,
                                           tb * 512:(tb + 1) * 512],
                            a16(kt_slot(oc, tbp), 512), waits=(("tv", EV),))
                prev_pj[bk] = EV
        for j in range(4):
            tau = tb * 4 + j
            for half in range(2):
                bk = half
                w0 = [("tv", max(prev_pj.get(bk, 0), prev_attev[0])),
                      ("tc", prev_attev[1]), ("tt", PV_of.get(tau - 4, 0))]
                if tau - 4 in VSPW:
                    w0.append(("dvs", VSPW[tau - 4]))
                MMD = 0
                for i in range(8):
                    MMD = top(lambda e, bk=bk, i=i, half=half, stage=stage, j=j: e.matmul(
                        bank(bk), a16(stage + j * 1024 + i * 128, 128),
                        a16(WV + i * 1024 + half * 512, 512),
                        start=(i == 0), stop=(i == 7)),
                        waits=(w0 if i == 0 else ()),
                        inc=("tt" if i == 7 else None))
                ve = VE + (tau % 4) * 1040
                vdst = (a16(ve, 1040).rearrange("p (h f) -> p h f", f=65)
                        [:, half * 8:(half + 1) * 8, 0:64])
                EV = vop(ts(vdst, bank(bk), 1.0, None, OP.mult), waits=(("tt", MMD),))
                prev_pj[bk] = EV
            ve = VE + (tau % 4) * 1040
            VED = vop(lambda e, ve=ve: e.memset(
                a16(ve, 1040).rearrange("p (h f) -> p h f", f=65)[:, :, 64:65], 1.0))
            VSPW[tau] = dma("gpsimd", v_sp[tau * 128:(tau + 1) * 128, :], a16(ve, 1040),
                            waits=(("tv", VED), ("dvs", C["dvs"])), sem="dvs")
        QKREADY = C["tv"]
        TBMM[tb] = C["tt"]
        if upto == "proj":
            continue

        for j in range(4):
            tau = tb * 4 + j
            pslot = tau % 2
            base = kt_slot(0, tbp) - KT
            vo = VO + (tau % 2) * 1040
            if tau == 0:
                vodl = sdma("sync", "dvo", A16[64:128, vo:vo + 1040], v_sp[0:64, :],
                            waits=(("dvs", VSPW[tau]), ("tv", CLR)))
            else:
                vodl = sdma("sync", "dvo", a16(vo, 1040),
                            v_sp[tau * 128 - 64: tau * 128 + 64, :],
                            waits=(("dvs", VSPW[tau]), ("tt", PV_of.get(tau - 2, 0))))
            first_qk = True
            QKD = 0
            for mloc, bnkE, bnkO in ((0, 2, 4), (1, 3, 5)):
                wcol = base + j * 128 + mloc * 64 - 64
                for hp in range(8):
                    for par, bnk in ((0, bnkE), (1, bnkO)):
                        w = (("tv", QKREADY), ("tc", prev_exp)) if first_qk else ()
                        first_qk = False
                        QKD = top(lambda e, hp=hp, wcol=wcol, bnk=bnk, j=j,
                                  tbp=tbp, par=par, mloc=mloc: e.matmul(
                            bank(bnk, 64, hp * 64),
                            A16[par * 64: par * 64 + 64,
                                KT + hp * 1152 + wcol: KT + hp * 1152 + wcol + 128],
                            A16[par * 64: par * 64 + 64,
                                QT + hp * 1024 + tbp * 512 + j * 128 + mloc * 64:
                                QT + hp * 1024 + tbp * 512 + j * 128 + mloc * 64 + 64],
                            start=True, stop=True, tile_position=(par * 64, 0)),
                            waits=w, inc=("tt" if hp == 7 else None))
            EXD = 0
            for qi, bnk in enumerate((2, 4, 3, 5)):
                EXD = scop(lambda e, qi=qi, bnk=bnk, pslot=pslot: e.activation(
                    out=a16(EXPT + pslot * 2048 + qi * 512, 512), in_=bank(bnk),
                    func=AF.Exp, bias=col(NEG8), scale=1.0),
                    waits=((("tt", QKD), ("dls", MLD),
                            ("dlg", DBGD.get(("ex", tau - 2), 0))) if qi == 0 else ()))
            prev_exp = EXD
            MKD = 0
            for half in range(2):
                moff = (MSK0 if (tau == 0 and half == 0) else MSKI)
                eap = (a16(EXPT + pslot * 2048 + half * 1024, 1024)
                       .rearrange("p (t f) -> p t f", f=64))
                msk = (a16(moff, 64).rearrange("p (o f) -> p o f", o=1)
                       .broadcast_to((128, 16, 64)))
                MKD = vop(lambda e, eap=eap, msk=msk: e.tensor_tensor(
                    out=eap, in0=eap, in1=msk, op=OP.mult),
                    waits=((("tc", EXD),) if half == 0 else ()))

            def att_slot(h):
                return (6, 65 * h) if h < 7 else ((7, 65 * (h - 7)) if h < 14
                                                  else (0, 65 * (h - 14)))
            if upto == "qkexp":
                prev_attev = (MKD, EXD)
                PV_of[tau] = QKD
                continue
            first_pv = True
            PVD = 0
            for h in range(H):
                hp, par = h // 2, h % 2
                for mloc in range(2):
                    qi = {(0, 0): 0, (0, 1): 1, (1, 0): 2, (1, 1): 3}[(mloc, par)]
                    ecol = EXPT + pslot * 2048 + qi * 512 + hp * 64
                    vtile = (VO + (tau % 2) * 1040) if mloc == 0 else (VE + (tau % 4) * 1040)
                    bnk, soff = att_slot(h)
                    w = ()
                    if first_pv:
                        w = [("tv", MKD), ("tv", prev_attev[0]), ("tc", prev_attev[1]),
                             ("dvo", vodl), ("dlg", DBGD.get(("at", tau - 2), 0))]
                        first_pv = False
                    PVD = top(lambda e, ecol=ecol, vtile=vtile, h=h, bnk=bnk,
                              soff=soff, mloc=mloc: e.matmul(
                        ps[64 * mloc: 64 * mloc + 64,
                           bnk * 512 + soff: bnk * 512 + soff + 65],
                        a16(ecol, 64), a16(vtile + h * 65, 65),
                        start=True, stop=True, tile_position=(0, 64 * mloc)),
                        waits=w, inc=("tt" if (h == H - 1 and mloc == 1) else None))
            PV_of[tau] = PVD
            if upto == "pv":
                prev_attev = (C["tv"], C["tc"])
                continue
            rc = RECIP + pslot * 16
            RCD = 0
            for gi, (bnk, nsl, rbase) in enumerate(((6, 7, 0), (7, 7, 7), (0, 2, 14))):
                RCD = vop(lambda e, bnk=bnk, nsl=nsl, rbase=rbase, rc=rc: e.reciprocal(
                    a32(rc + rbase, nsl),
                    ps[:, bnk * 512 + 64: bnk * 512 + 64 + (nsl - 1) * 65 + 1: 65]),
                    waits=((("tt", PVD),) if gi == 0 else ()))
            attb = ATT + pslot * 1024
            AEV, AEC = 0, 0
            for h in list(range(0, 7)) + [14, 15]:
                bnk, soff = att_slot(h)
                AEV = vop(ts(a16(attb + h * 64, 64),
                             ps[:, bnk * 512 + soff: bnk * 512 + soff + 64],
                             a32(rc + h, 1), None, OP.mult),
                          waits=((("tv", RCD),) if h == 0 else ()))
            for h in range(7, 14):
                bnk, soff = att_slot(h)
                AEC = scop(lambda e, h=h, bnk=bnk, soff=soff, attb=attb, rc=rc: e.activation(
                    out=a16(attb + h * 64, 64),
                    in_=ps[:, bnk * 512 + soff: bnk * 512 + soff + 64],
                    func=AF.Identity, bias=0.0, scale=a32(rc + h, 1)),
                    waits=(("tv", RCD),))
            prev_attev = (AEV, AEC)
            if upto == "evac":
                continue
            if "att" in sbuf_dbg:
                DBGD[("at", tau)] = dma("gpsimd",
                    sbuf_dbg["att"][tau * 128:(tau + 1) * 128, :],
                    a16(attb, 1024), waits=(("tv", AEV), ("tc", AEC)))
            if "expt" in sbuf_dbg:
                DBGD[("ex", tau)] = dma("gpsimd",
                    sbuf_dbg["expt"][tau * 16 * 128: (tau + 1) * 16 * 128, :]
                    .rearrange("(a p) f -> p a f", p=128),
                    a16(EXPT + pslot * 2048, 2048).rearrange("p (a f) -> p a f", f=128),
                    waits=(("tv", MKD),))
            xs = XR + (tau % 4) * 1024
            attn = a16(XN + (tau % 2) * 1024, 1024)
            AND = ln_norm(a16(attb, 1024), attn, extra_wait=(("tc", AEC), ("tv", AEV)),
                          dst_waits=(("dlg", DBGD.get(("xn", tau), 0)),))
            x1b = X1B + (tau % 2) * 1024
            X1V = vop(lambda e, x1b=x1b, xs=xs, attn=attn: e.tensor_tensor(
                out=a32(x1b, 1024), in0=a32(xs, 1024), in1=attn, op=OP.add),
                waits=(("tc", AND), ("dlg", X1SP.get(tau - 2, 0))))
            X1D[tau] = X1V
            X1SP[tau] = dma("gpsimd", x1_sp[tau * 128:(tau + 1) * 128, :], a32(x1b, 1024),
                            waits=(("tv", X1V),))
            xn2 = a16(XN2 + (tau % 2) * 1024, 1024)
            XN2D = ln_norm(a32(x1b, 1024), xn2,
                           dst_waits=(("dlt", T2D_of.get(tau - 2, 0)),))
            scr_a = a16(XN + ((tau + 1) % 2) * 1024, 1024)
            vop(ts(scr_a, xn2, 1.0, None, OP.mult, OP.max, a32(MXA + 2 * (tau % 64), 1)),
                waits=(("tc", XN2D),))
            MXD = vop(ts(scr_a, xn2, -1.0, None, OP.mult, OP.max,
                         a32(MXA + 2 * (tau % 64) + 1, 1)))
            x2t = a16(XN2T + (tau % 2) * 1024, 1024)
            T2D = dmat(x2t.rearrange("p (c f) -> p c f", f=128), xn2,
                       waits=(("tc", XN2D), ("dlg", XSPILL.get(tau - 2, 0))))
            T2D_of[tau] = T2D
            XSPILL[tau] = dma("gpsimd",
                xn2t_sp[:, tau * 128:(tau + 1) * 128].rearrange("(c p) f -> p c f", p=128),
                x2t.rearrange("p (c f) -> p c f", f=128),
                waits=(("dlt", T2D), ("tv", MXD)))
    ATTV, ATTC, ATTG = C["tv"], C["tc"], C["dlg"]
    RUN_AR1 = RUN_ATTN and upto not in ("attn", "proj", "qkexp", "pv", "evac")

    sc_slot = [0]

    def fold_reduce(src_off, nsl, out_off, waits):
        FV = vop(ts(a16(SCR, nsl), a32(src_off, nsl), 1.0, None, OP.mult, OP.max,
                    col(SRED + 256)), waits=waits)
        RG = dma("scalar", red_sp[0:1, :], col(SRED + 256), waits=(("tv", FV),))
        RL = dma("sync", A32[0:1, SRED: SRED + 128], red_sp[0:1, :],
                 waits=(("dlt", RG),))
        return vop(lambda e, out_off=out_off: e.tensor_reduce(
            out=A32[0:1, out_off: out_off + 1], in_=A32[0:1, SRED: SRED + 128],
            axis=AX.X, op=OP.max, apply_absolute_value=False),
            waits=(("dls", RL),), selfw=True)

    def collective(idx, slot_srcs, waits):
        VZ = vop(lambda e: e.memset(A32[0:1, SRED + 304: SRED + 320], 0.0), waits=waits)
        for slot, off in slot_srcs:
            VZ = vop(lambda e, slot=slot, off=off: e.tensor_copy(
                A32[0:1, SRED + 304 + slot: SRED + 305 + slot],
                A32[0:1, off: off + 1]), selfw=True)
        W = dma("scalar", ar_in[idx][:, :], A32[0:1, SRED + 304: SRED + 320],
                waits=(("tv", VZ),))
        sch.op("gpsimd", lambda e, idx=idx: e.collective_compute(
            "AllReduce", OP.max, replica_groups=[list(range(ncores))],
            ins=[ar_in[idx].ap().opt()], outs=[ar_out[idx].ap().opt()]),
            waits=(("dlt", W),), inc="cc", k=1)
        return dma("sync", A32[0:1, SRED + 352: SRED + 368], ar_out[idx][:, :],
                   waits=(("cc", C["cc"]),))

    def bcast(dst_col, src_off, waits):
        slt = sc_slot[0] % 32
        sc_slot[0] += 1
        W = dma("scalar", sc_sp[0:1, slt:slt + 1], A32[0:1, src_off: src_off + 1],
                waits=waits)
        return dma("scalar", col(dst_col), sc_sp[0:1, slt:slt + 1].partition_broadcast(128),
                   waits=(("dlt", W),))

    if not RUN_AR1:
        _dbg_copy(dbg_outs, dma, a16, a32, OB, C, F32)
        sch.op("gpsimd", None, waits=(("dlg", C["dlg"]),))
        return _finish(nc, sch, sems, stack)
    FR = fold_reduce(MXA, 128, SRED + 300,
                     waits=(("tv", ATTV), ("tc", ATTC), ("dlg", ATTG)))
    ARL = collective(0, [(0, SRED + 300)], waits=(("tv", FR),))
    vop(ts(A32[0:1, SRED + 400: SRED + 401], A32[0:1, SRED + 352: SRED + 353],
           1.0 / QMAX, 1e-12, OP.mult, OP.max), waits=(("dls", ARL),))
    SCV = vop(lambda e: e.reciprocal(A32[0:1, SRED + 401: SRED + 402],
                                     A32[0:1, SRED + 400: SRED + 401]), selfw=True)
    bcast(SCAL + 1, SRED + 400, waits=(("tv", SCV),))
    L_rs1 = bcast(SCAL + 0, SRED + 401, waits=(("tv", SCV),))

    prev_f = {}

    def ffn_pass_gen(wT, spill, ready_dls):
        mxn = 0
        prev_sb_mm = C["tt"]
        for sbi in range(NSB):
            XQD = 0
            for dc in range(8):
                buf = dc % 2
                LD = sdma("sync", "dxq", a16(XQS + buf * 2048, SBT),
                          xn2t_sp[dc * 128:(dc + 1) * 128, sbi * SBT:(sbi + 1) * SBT],
                          waits=(("tv", prev_f.get(("ld", buf), 0)), ("dlt", ready_dls),
                                 ("dlg", ATTG), ("tv", ATTV), ("tc", ATTC)))
                vop(ts(ai(0, SBT), a16(XQS + buf * 2048, SBT), col(SCAL + 0), QMAX,
                       OP.mult, OP.min), waits=(("dxq", LD),))
                XQD = vop(ts(a16(XQ + dc * SBT, SBT), ai(0, SBT), 1.0, None, OP.mult),
                          waits=(("tt", prev_sb_mm),
                                 ("dlg", DBGD.get(("xq", dc, sbi - 1), 0))), selfw=True)
                prev_f[("ld", buf)] = XQD
                if "xq" in sbuf_dbg:
                    DBGD[("xq", dc, sbi)] = dma("gpsimd",
                        sbuf_dbg["xq"][dc * 128:(dc + 1) * 128, sbi * SBT:(sbi + 1) * SBT],
                        a16(XQ + dc * SBT, SBT), waits=(("tv", XQD),))
            for oc in range(24):
                wb = oc % 2
                WL = sdma("sync", "dws",
                          a16(WST + wb * 1024, 1024).rearrange("p (i f) -> p i f", f=128),
                          wT[oc * 1024:(oc + 1) * 1024, :].rearrange("(i p) f -> p i f", p=128),
                          waits=(("tt", prev_f.get(("wst", wb), 0)),))
                for tb4 in range(SBT // 512):
                    bk = (oc % 2) * 4 + tb4
                    w0 = [("dws", WL), ("tv", XQD),
                          ("tc", prev_f.get(("fbc", bk), 0))]
                    MMD = 0
                    for i in range(8):
                        MMD = top(lambda e, bk=bk, wb=wb, i=i, tb4=tb4: e.matmul(
                            bank(bk), a16(WST + wb * 1024 + i * 128, 128),
                            a16(XQ + i * SBT + tb4 * 512, 512),
                            start=(i == 0), stop=(i == 7)),
                            waits=(w0 if i == 0 else ()),
                            inc=("tt" if i == 7 else None))
                    if tb4 == SBT // 512 - 1:
                        prev_f[("wst", wb)] = MMD
                    prev_sb_mm = MMD
                    ev = EVS + (oc % 4) * 2048 + tb4 * 512
                    ECD = scop(lambda e, ev=ev, bk=bk: e.activation(
                        out=a16(ev, 512), in_=bank(bk), func=AF.Identity, bias=0.0,
                        scale=col(SCAL + 1)),
                        waits=(("tt", MMD), ("dlg", prev_f.get(("sp", oc % 4, tb4), 0)),
                               ("tv", prev_f.get(("fbv2", oc % 4), 0))))
                    prev_f[("fbc", bk)] = ECD
                    SPD = dma("gpsimd",
                              spill[oc * 128:(oc + 1) * 128,
                                    sbi * SBT + tb4 * 512: sbi * SBT + (tb4 + 1) * 512],
                              a16(ev, 512), waits=(("tc", ECD),))
                    prev_f[("sp", oc % 4, tb4)] = SPD
                slot = MXF + (mxn % 768)
                mxn += 1
                MXO = vop(lambda e, oc=oc, slot=slot: e.tensor_reduce(
                    out=a32(slot, 1), in_=a16(EVS + (oc % 4) * 2048, SBT),
                    axis=AX.X, op=OP.max, apply_absolute_value=True),
                    waits=(("tc", ECD),))
                prev_f[("fbv2", oc % 4)] = MXO
                yield
        yield

    def ffn_pass(wT, spill, ready_dls, inter=None):
        g = ffn_pass_gen(wT, spill, ready_dls)
        for _ in g:
            if inter is not None:
                try:
                    next(inter)
                except StopIteration:
                    inter = None
        if inter is not None:
            for _ in inter:
                pass
        return C["tv"], C["dlg"]

    MXCLR = vop(lambda e: e.memset(a32(MXF, 1024), 0.0), waits=(("tv", ATTV),))
    if upto == "ar1":
        sch.op("gpsimd", None, waits=(("dlg", C["dlg"]), ("dls", C["dls"]), ("tv", C["tv"])))
        return _finish(nc, sch, sems, stack)
    GV, GG = ffn_pass(wgT_sp, gt_sp, L_rs1)
    if upto == "gate":
        _dbg_copy(dbg_outs, dma, a16, a32, OB, C, F32)
        sch.op("gpsimd", None, waits=(("dlg", C["dlg"]), ("dls", C["dls"]), ("tv", C["tv"]), ("tc", C["tc"])))
        return _finish(nc, sch, sems, stack)
    FR = fold_reduce(MXF, 768, SRED + 300, waits=(("tv", GV),))
    ARL2 = collective(1, [(0, SRED + 300)], waits=(("tv", FR), ("dlg", GG)))
    vop(ts(A32[0:1, SRED + 404: SRED + 405], A32[0:1, SRED + 352: SRED + 353],
           1.0 / QMAX, 1e-12, OP.mult, OP.max), waits=(("dls", ARL2),))
    vop(lambda e: e.reciprocal(A32[0:1, SRED + 405: SRED + 406],
                               A32[0:1, SRED + 404: SRED + 405]), selfw=True)
    SCV = vop(ts(A32[0:1, SRED + 406: SRED + 407], A32[0:1, SRED + 404: SRED + 405],
                 1.0 / 65536.0, None, OP.mult), selfw=True)
    bcast(SCAL + 2, SRED + 405, waits=(("tv", SCV),))
    bcast(SCAL + 3, SRED + 404, waits=(("tv", SCV),))
    L_s3 = bcast(SCAL + 4, SRED + 406, waits=(("tv", SCV),))
    MXCLR2 = vop(lambda e: e.memset(a32(MXF, 768), 0.0), waits=(("tv", GV),))

    def p1_gen():
        pc = {}
        for ct in range(24 * NCB):
            dc, cb = ct // NCB, ct % NCB
            buf = ct % 2
            gsl = a16(LD1 + buf * 2048, CW)
            LD = sdma("sync", "dch", gsl,
                      gt_sp[dc * 128:(dc + 1) * 128, cb * CW:(cb + 1) * CW],
                      waits=(("tc", pc.get(("y1", buf), 0)), ("dlt", L_s3)))
            Y1D = scop(lambda e, gsl=gsl: e.activation(
                out=ai(2048, CW), in_=gsl, func=AF.Identity, bias=0.0,
                scale=col(SCAL + 2)),
                waits=(("dch", LD), ("tv", pc.get("stt", 0))))
            pc[("y1", buf)] = Y1D
            sg = a16(SG2 + buf * 2048, CW)
            SGD = scop(lambda e, sg=sg: e.activation(
                out=sg, in_=ai(2048, CW), func=AF.Sigmoid, bias=0.0,
                scale=col(SCAL + 3)),
                waits=(("tv", pc.get(("ysq", buf), 0)),), selfw=True)
            YG1F = scop(lambda e: e.activation(
                out=a32(0, CW), in_=ai(2048, CW), func=AF.Identity, bias=0.0,
                scale=col(SCAL + 4)),
                waits=(("tv", pc.get("stt", ATTV)),), selfw=True)
            YSQ = vop(ts(ai(2048, CW), sg, 65536.0, 65535.0, OP.mult, OP.min),
                      waits=(("tc", YG1F),))
            pc[("ysq", buf)] = YSQ
            tt_ = a16(TO3 + buf * 2048, CW)
            vop(lambda e, tt_=tt_: e.scalar_tensor_tensor(
                out=tt_, in0=a32(0, CW), scalar=1.0, in1=ai(2048, CW),
                op0=OP.mult, op1=OP.mult),
                waits=(("dlg", pc.get(("sp", buf), 0)),), selfw=True)
            pc["stt"] = C["tv"]
            TTD = vop(lambda e, tt_=tt_, ct=ct: e.tensor_reduce(
                out=a32(MXF + 768 + ct, 1), in_=tt_, axis=AX.X, op=OP.max,
                apply_absolute_value=True), selfw=True)
            pc[("sp", buf)] = dma("gpsimd",
                t_sp[dc * 128:(dc + 1) * 128, cb * CW:(cb + 1) * CW], tt_,
                waits=(("tv", TTD),))
            yield

    UV, UG = ffn_pass(wuT_sp, up_sp, L_s3, inter=p1_gen())
    if upto == "up":
        _dbg_copy(dbg_outs, dma, a16, a32, OB, C, F32)
        sch.op("gpsimd", None, waits=(("dlg", C["dlg"]), ("dls", C["dls"]), ("tv", C["tv"]), ("tc", C["tc"])))
        return _finish(nc, sch, sems, stack)

    P1V, P1G, P1C = C["tv"], C["dlg"], C["tc"]
    if upto == "p1":
        _dbg_copy(dbg_outs, dma, a16, a32, OB, C, F32)
        sch.op("gpsimd", None, waits=(("dlg", C["dlg"]), ("dls", C["dls"]), ("tv", C["tv"]), ("tc", C["tc"])))
        return _finish(nc, sch, sems, stack)

    FRa = fold_reduce(MXF, 768, SRED + 300, waits=(("tv", max(UV, P1V)),))
    FRb = fold_reduce(MXF + 768, 256, SRED + 301, waits=(("tv", FRa),))
    ARL3 = collective(2, [(0, SRED + 300), (1, SRED + 301)],
                      waits=(("tv", FRb), ("dlg", max(UG, P1G))))
    vop(ts(A32[0:1, SRED + 408: SRED + 409], A32[0:1, SRED + 352: SRED + 353],
           1.0 / QMAX, 1e-12, OP.mult, OP.max), waits=(("dls", ARL3),))
    vop(lambda e: e.reciprocal(A32[0:1, SRED + 409: SRED + 410],
                               A32[0:1, SRED + 408: SRED + 409]), selfw=True)
    vop(ts(A32[0:1, SRED + 410: SRED + 411], A32[0:1, SRED + 353: SRED + 354],
           1.0 / QMAX, 1e-12, OP.mult, OP.max))
    vop(lambda e: e.reciprocal(A32[0:1, SRED + 411: SRED + 412],
                               A32[0:1, SRED + 410: SRED + 411]), selfw=True)
    SCV = vop(lambda e: e.tensor_tensor(
        out=A32[0:1, SRED + 412: SRED + 413], in0=A32[0:1, SRED + 408: SRED + 409],
        in1=A32[0:1, SRED + 410: SRED + 411], op=OP.mult), selfw=True)
    bcast(SCAL + 5, SRED + 409, waits=(("tv", SCV),))
    bcast(SCAL + 6, SRED + 411, waits=(("tv", SCV),))
    L_p2 = bcast(SCAL + 7, SRED + 412, waits=(("tv", SCV),))
    MXCLR3 = vop(lambda e: e.memset(a32(MXF, 1024), 0.0), waits=(("tv", max(UV, P1V)),))

    prev_c = {}
    HW_ = min(1024, CW)
    NH = CW // HW_
    for ct in range(24 * NCB):
        dc, cb = ct // NCB, ct % NCB
        b4 = ct % 4
        tsl = a16(XQ + b4 * 2048, CW)
        usl = a16(WST + b4 * 2048, CW)
        uu = a16(LD1 + b4 * 2048, CW)
        LT = sdma("sync", "dch", tsl,
                  t_sp[dc * 128:(dc + 1) * 128, cb * CW:(cb + 1) * CW],
                  waits=(("tc", prev_c.get(("y2", b4), P1C)), ("dlt", L_p2)))
        LU = sdma("scalar", "dup", usl,
                  up_sp[dc * 128:(dc + 1) * 128, cb * CW:(cb + 1) * CW],
                  waits=(("tc", prev_c.get(("yu", b4), P1C)),
                         ("tv", P1V if ct < 4 else 0),
                         ("dlg", P1G if ct < 4 else 0)))
        Y2D = YUD = 0
        for h in range(NH):
            qa = (ct * NH + h) % 2
            Y2D = scop(lambda e, tsl=tsl, h=h, qa=qa: e.activation(
                out=ai(qa * 1024, HW_), in_=tsl[:, h * HW_:(h + 1) * HW_],
                func=AF.Identity, bias=0.0, scale=col(SCAL + 6)),
                waits=(("dch", LT), prev_c.get(("sttq", qa), ("tv", P1V))))
            YUD = scop(lambda e, usl=usl, h=h, qa=qa: e.activation(
                out=ai(2048 + qa * 1024, HW_), in_=usl[:, h * HW_:(h + 1) * HW_],
                func=AF.Identity, bias=0.0, scale=col(SCAL + 5)),
                waits=(("dup", LU),), selfw=True)
            STT = vop(lambda e, uu=uu, h=h, qa=qa: e.scalar_tensor_tensor(
                out=uu[:, h * HW_:(h + 1) * HW_], in0=ai(qa * 1024, HW_),
                scalar=col(SCAL + 7), in1=ai(2048 + qa * 1024, HW_),
                op0=OP.mult, op1=OP.mult),
                waits=(("tc", YUD), prev_c.get(("sp", b4), ("dlg", P1G))),
                selfw=True)
            prev_c[("sttq", qa)] = ("tv", STT)
            RD = vop(lambda e, uu=uu, h=h, ct=ct: e.tensor_reduce(
                out=a32(MXF + (ct * NH + h) % 768, 1),
                in_=uu[:, h * HW_:(h + 1) * HW_], axis=AX.X, op=OP.max,
                apply_absolute_value=True), selfw=True)
            SPU = 0
            for k8 in range(h * (HW_ // 256), (h + 1) * (HW_ // 256)):
                tbk = cb * (CW // 256) + k8
                SPU = dma("gpsimd",
                          u_sp[tbk * I3 + dc * 128: tbk * I3 + (dc + 1) * 128, :],
                          uu[:, k8 * 256:(k8 + 1) * 256], waits=(("tv", RD),))
        prev_c[("y2", b4)] = Y2D
        prev_c[("yu", b4)] = YUD
        prev_c[("sp", b4)] = ("dlg", SPU)
    P2V, P2G, P2S = C["tv"], C["dlg"], C["dls"]
    if upto == "p2":
        _dbg_copy(dbg_outs, dma, a16, a32, OB, C, F32)
        sch.op("gpsimd", None, waits=(("dlg", C["dlg"]), ("dls", C["dls"]), ("tv", C["tv"]), ("tc", C["tc"])))
        return _finish(nc, sch, sems, stack)

    FR = fold_reduce(MXF, 768, SRED + 300, waits=(("tv", P2V),))
    ARL4 = collective(3, [(0, SRED + 300)],
                      waits=(("tv", FR), ("dlg", P2G), ("dls", P2S)))
    vop(ts(A32[0:1, SRED + 414: SRED + 415], A32[0:1, SRED + 352: SRED + 353],
           1.0 / QMAX, 1e-12, OP.mult, OP.max), waits=(("dls", ARL4),))
    SCV = vop(lambda e: e.reciprocal(A32[0:1, SRED + 415: SRED + 416],
                                     A32[0:1, SRED + 414: SRED + 415]), selfw=True)
    bcast(SCAL + 8, SRED + 415, waits=(("tv", SCV),))
    L_p8 = bcast(SCAL + 9, SRED + 414, waits=(("tv", SCV),))

    WDL = 0
    wdT_all = wdT_sp[:, :].rearrange("(jc icc p) f -> p jc icc f", p=128, icc=24)
    for icc in range(24):
        WDL = sdma("sync", "dwd",
                   a16(WDT + icc * 1024, 1024).rearrange("p (i f) -> p i f", f=128),
                   wdT_all[:, :, icc, :],
                   waits=(("tv", P2V), ("tc", C["tc"])))
    prev_p8 = {}
    NBLK = S // 256
    ODMA = 0
    TTF = C["tt"]
    uinfo = {}
    for b in range(NBLK + 1):
        if b < NBLK:
            us = 12288 + (b % 2) * 6144
            ob2 = (b % 2) * 6144
            ULD = sdma("sync", "dul",
                       a16(us, 6144).rearrange("p (i f) -> p i f", f=256),
                       u_sp[b * I3: (b + 1) * I3, :].rearrange("(i p) f -> p i f", p=128),
                       waits=(("tc", prev_p8.get(("ust", b % 2), 0)), ("dlt", L_p8)))
            OBD = 0
            for hh in range(2):
                YOD = scop(lambda e, hh=hh, us=us: e.activation(
                    out=ai(0, 3072), in_=a16(us + hh * 3072, 3072), func=AF.Identity,
                    bias=0.0, scale=col(SCAL + 8)),
                    waits=((("dul", ULD), ("tv", P2V if b == 0 else 0))
                           if hh == 0 else ()),
                    selfw=True)
                prev_p8[("ust", b % 2)] = YOD
                OBD = scop(lambda e, hh=hh, ob2=ob2: e.activation(
                    out=a16(ob2 + hh * 3072, 3072), in_=ai(0, 3072), func=AF.Identity,
                    bias=0.0, scale=1.0),
                    waits=((("tt", prev_p8.get(("mmb", b % 2), TTF)),)
                           if hh == 0 else ()),
                    selfw=True)
            uinfo[b] = (OBD, ob2)
        if b == 0:
            continue
        bb = b - 1
        OBDp, obp = uinfo.pop(bb)
        grp = (bb % 2) * 4
        MMD = 0
        for ti in range(2):
            tau = bb * 2 + ti
            for jh in range(2):
                bk = grp + ti * 2 + jh
                w0 = [("tc", OBDp), ("dwd", WDL),
                      ("tv", prev_p8.get(("ev", bk), 0)), ("tc", prev_p8.get(("ec", bk), 0))]
                MMD = 0
                for icc in range(24):
                    MMD = top(lambda e, bk=bk, icc=icc, ti=ti, jh=jh, obp=obp: e.matmul(
                        bank(bk),
                        a16(obp, 6144).rearrange("p (i f) -> p i f", f=256)
                        [:, icc, ti * 128:(ti + 1) * 128],
                        a16(WDT + icc * 1024 + jh * 512, 512),
                        start=(icc == 0), stop=(icc == 23)),
                        waits=(w0 if icc == 0 else ()),
                        inc=("tt" if icc == 23 else None))
                ECD = scop(lambda e, bk=bk, ti=ti, jh=jh: e.activation(
                    out=a16(FFT + ti * 1024 + jh * 512, 512), in_=bank(bk),
                    func=AF.Identity, bias=0.0, scale=col(SCAL + 9)),
                    waits=(("tt", MMD), ("dlg", DBGD.get(("ff", tau - 2), 0))))
                prev_p8[("ec", bk)] = ECD
            fft = a16(FFT + ti * 1024, 1024)
            if "ffn" in sbuf_dbg:
                DBGD[("ff", tau)] = dma("gpsimd",
                    sbuf_dbg["ffn"][tau * 128:(tau + 1) * 128, :], fft,
                    waits=(("tc", ECD),))
            ffn_n = a16(FFN_N + ti * 1024, 1024)
            x1r = XR + (tau % 2) * 1024
            XLD = sdma("sync", "dxr", a32(x1r, 1024), x1_sp[tau * 128:(tau + 1) * 128, :],
                       waits=(("tv", prev_p8.get(("x1", tau % 2), P1V)),))
            FND = ln_norm(fft, ffn_n, extra_wait=(("tc", ECD),))
            ob_ = 2048 + (tau % 2) * 1024
            OV = vop(lambda e, ob_=ob_, x1r=x1r, ffn_n=ffn_n: e.tensor_tensor(
                out=a32(ob_, 1024), in0=a32(x1r, 1024), in1=ffn_n, op=OP.add),
                waits=(("tc", FND), ("dxr", XLD),
                       ("dlg", prev_p8.get(("od", tau % 2), 0))))
            prev_p8[("x1", tau % 2)] = OV
            for jh in range(2):
                prev_p8[("ev", grp + ti * 2 + jh)] = OV
            ODMA = dma("gpsimd", out_d[tau * 128:(tau + 1) * 128, :], a32(ob_, 1024),
                       waits=(("tv", OV),))
            prev_p8[("od", tau % 2)] = ODMA
        prev_p8[("mmb", bb % 2)] = MMD

    _dbg_copy(dbg_outs, dma, a16, a32, OB, C, F32)
    sch.op("gpsimd", None, waits=(("dlg", C["dlg"]),))
    return _finish(nc, sch, sems, stack)


def _dbg_copy(dbg_outs, dma, a16, a32, OB, C, F32):
    first = True
    ping = [0]
    for name, (dst, srct) in dbg_outs.items():
        rows, cols = dst.ap().shape
        isf32 = dst.ap().dtype == F32
        cmax = 2048 if isf32 else 8192
        for r0 in range(0, rows, 128):
            rr = min(128, rows - r0)
            for c0 in range(0, cols, cmax):
                cc = min(cmax, cols - c0)
                if isf32:
                    slot = (ping[0] % 2) * 2048
                    stg_full = a32(slot, cc)
                else:
                    slot = OB + (ping[0] % 2) * 8192
                    stg_full = a16(slot, cc)
                ping[0] += 1
                stg = stg_full if rr == 128 else stg_full[0:rr, :]
                w = [("dls", C["dls"]), ("dlg", C["dlg"])]
                if first:
                    w += [("tv", C["tv"]), ("tc", C["tc"])]
                    first = False
                LD = dma("sync", stg, srct[r0:r0 + rr, c0:c0 + cc], waits=tuple(w))
                dma("gpsimd", dst[r0:r0 + rr, c0:c0 + cc], stg, waits=(("dls", LD),))


def _finish(nc, sch, sems, stack):
    with nc.Block() as block:
        @block.sync
        def _(sync):
            sch.run("sync", sync, sems)

        @block.scalar
        def _(scalar):
            sch.run("scalar", scalar, sems)

        @block.vector
        def _(vector):
            sch.run("vector", vector, sems)

        @block.tensor
        def _(tensor):
            sch.run("tensor", tensor, sems)

        @block.gpsimd
        def _(gpsimd):
            sch.run("gpsimd", gpsimd, sems)

    stack.close()
    return nc


def make_masks():
    tri = (np.arange(CHK)[:, None] <= np.arange(CHK)[None, :]).astype(np.float32)
    mski = np.ones((128, CHK), np.float32)
    mski[64:128, :] = tri
    msk0 = np.zeros((128, CHK), np.float32)
    msk0[64:128, :] = tri
    return mski.astype(ml_dtypes.bfloat16), msk0.astype(ml_dtypes.bfloat16)


def kernel(L0_attn_hidden_states, L0_attn_Wq, L0_attn_Wk, L0_attn_Wv,
           L0_ffn_w_gate, L0_ffn_w_up, L0_ffn_w_down):
    x = np.ascontiguousarray(np.asarray(L0_attn_hidden_states, dtype=np.float32))
    B, S_, D_ = x.shape
    assert B == NCORES and D_ == DM
    mski, msk0 = make_masks()
    common = dict(
        wq=np.ascontiguousarray(np.asarray(L0_attn_Wq, np.float32)),
        wk=np.ascontiguousarray(np.asarray(L0_attn_Wk, np.float32)),
        wv=np.ascontiguousarray(np.asarray(L0_attn_Wv, np.float32)),
        wg=np.ascontiguousarray(np.asarray(L0_ffn_w_gate, np.float32)),
        wu=np.ascontiguousarray(np.asarray(L0_ffn_w_up, np.float32)),
        wd=np.ascontiguousarray(np.asarray(L0_ffn_w_down, np.float32)),
        mski=mski, msk0=msk0,
    )
    nc = build_nc(S=S_, ncores=NCORES)
    in_maps = [dict(common, x=np.ascontiguousarray(x[c])) for c in range(NCORES)]
    res = run_bass_kernel_spmd(nc, in_maps, core_ids=list(range(NCORES)))
    out = np.stack([np.asarray(res.results[c]["out"]) for c in range(NCORES)], axis=0)
    return out.astype(np.float32)


# version 07 (final)
# speedup vs baseline: 1.2869x; 1.0067x over previous
import numpy as np
import ml_dtypes
import concourse.bass as bass
import concourse.mybir as mybir
from concourse.bass_utils import run_bass_kernel_spmd

F32 = mybir.dt.float32
BF16 = mybir.dt.bfloat16
I32 = mybir.dt.int32
AF = mybir.ActivationFunctionType
OP = mybir.AluOpType
AX = mybir.AxisListType

NCORES = 8
DM = 1024
H = 16
DH = 64
CHK = 64
I3 = 3072
QMAX = 32767.0
EPS = 1e-5
ENGS = ("sync", "scalar", "vector", "tensor", "gpsimd")


class Sched:
    def __init__(self, strict=False):
        self.ops = {e: [] for e in ENGS}
        self.cnt = {s: 0 for s in ('dls', 'dlt', 'dlg', 'dvs', 'dvo', 'dxl', 'dws', 'dxq', 'dch', 'dul', 'dwd', 'dxr', 'dpl', 'dup', 'tt', 'tv', 'tc', 'tg', 'cc')}
        self.strict = strict
        self.last = {e: None for e in ENGS}

    def op(self, eng, fn, waits=(), inc=None, k=1):
        if self.strict and self.last[eng] is not None:
            waits = tuple(waits) + (self.last[eng],)
        if inc is not None:
            self.cnt[inc] += k
            self.last[eng] = (inc, self.cnt[inc])
        self.ops[eng].append((fn, tuple(w for w in waits if w[1] > 0), inc, k))

    def run(self, eng_name, engobj, sems):
        for fn, waits, inc, k in self.ops[eng_name]:
            for s, v in waits:
                engobj.wait_ge(sems[s], v)
            if fn is not None:
                ins = fn(engobj)
                if inc is not None:
                    ins.then_inc(sems[inc], k)


def build_nc(S=8192, ncores=NCORES, dbg=(), strict=False, upto="full"):
    NTB = S // 512
    SBT = min(2048, S)
    NSB = S // SBT
    CW = min(2048, S)
    NCB = S // CW

    nc = bass.Bass("TRN2", target_bir_lowering=False, debug=False,
                   detect_race_conditions=strict)

    x_d = nc.dram_tensor("x", [S, DM], F32, kind="ExternalInput")
    wq_d = nc.dram_tensor("wq", [DM, DM], F32, kind="ExternalInput")
    wk_d = nc.dram_tensor("wk", [DM, DM], F32, kind="ExternalInput")
    wv_d = nc.dram_tensor("wv", [DM, DM], F32, kind="ExternalInput")
    wg_d = nc.dram_tensor("wg", [I3, DM], F32, kind="ExternalInput")
    wu_d = nc.dram_tensor("wu", [I3, DM], F32, kind="ExternalInput")
    wd_d = nc.dram_tensor("wd", [DM, I3], F32, kind="ExternalInput")
    mski_d = nc.dram_tensor("mski", [128, CHK], BF16, kind="ExternalInput")
    msk0_d = nc.dram_tensor("msk0", [128, CHK], BF16, kind="ExternalInput")
    out_d = nc.dram_tensor("out", [S, DM], F32, kind="ExternalOutput")

    x1_sp = nc.dram_tensor("x1_sp", [S, DM], F32)
    xn2t_sp = nc.dram_tensor("xn2t_sp", [DM, S], BF16)
    v_sp = nc.dram_tensor("v_sp", [S, H * (DH + 1)], BF16)
    wgT_sp = nc.dram_tensor("wgT_sp", [24 * 8 * 128, 128], BF16)
    wuT_sp = nc.dram_tensor("wuT_sp", [24 * 8 * 128, 128], BF16)
    wdT_sp = nc.dram_tensor("wdT_sp", [8 * 24 * 128, 128], BF16)
    gt_sp = nc.dram_tensor("gt_sp", [I3, S], BF16)
    up_sp = nc.dram_tensor("up_sp", [I3, S], BF16)
    t_sp = nc.dram_tensor("t_sp", [I3, S], BF16)
    u_sp = nc.dram_tensor("u_sp", [(S // 256) * I3, 256], BF16)
    red_sp = nc.dram_tensor("red_sp", [1, 128], F32)
    sc_sp = nc.dram_tensor("sc_sp", [1, 32], F32)
    ar_in = [nc.dram_tensor(f"ar_in{i}", [1, 16], F32) for i in range(4)]
    ar_out = [nc.dram_tensor(f"ar_out{i}", [1, 16], F32) for i in range(4)]

    dbg_specs = {
        "wgT": (wgT_sp, [24 * 8 * 128, 128], BF16),
        "wuT": (wuT_sp, [24 * 8 * 128, 128], BF16),
        "wdT": (wdT_sp, [8 * 24 * 128, 128], BF16),
        "x1": (x1_sp, [S, DM], F32), "xn2t": (xn2t_sp, [DM, S], BF16),
        "gt": (gt_sp, [I3, S], BF16), "up": (up_sp, [I3, S], BF16),
        "t": (t_sp, [I3, S], BF16), "u": (u_sp, [(S // 256) * I3, 256], BF16),
        "vsp": (v_sp, [S, H * (DH + 1)], BF16),
        "ar0": (ar_out[0], [1, 16], F32), "ar1": (ar_out[1], [1, 16], F32),
        "ar2": (ar_out[2], [1, 16], F32), "ar3": (ar_out[3], [1, 16], F32),
    }
    sbuf_dbg = {}
    for name in dbg:
        if name in ("xn", "att", "ffn"):
            sbuf_dbg[name] = nc.dram_tensor("dbg_" + name, [S, DM], BF16, kind="ExternalOutput")
        elif name in ("qt", "kt", "xq"):
            sbuf_dbg[name] = nc.dram_tensor("dbg_" + name, [DM, S], BF16, kind="ExternalOutput")
        elif name == "expt":
            sbuf_dbg[name] = nc.dram_tensor("dbg_expt", [S * 16, 128], BF16, kind="ExternalOutput")
    dbg_outs = {}
    for name in dbg:
        if name in sbuf_dbg:
            continue
        src, shp, dt = dbg_specs[name]
        dbg_outs[name] = (nc.dram_tensor("dbg_" + name, shp, dt, kind="ExternalOutput"), src)

    sch = Sched(strict=strict)
    C = sch.cnt

    from contextlib import ExitStack
    stack = ExitStack()
    A16 = stack.enter_context(nc.sbuf_tensor("A16", [128, 70656], BF16))
    A32 = stack.enter_context(nc.sbuf_tensor("A32", [128, 8192], F32))
    AI = stack.enter_context(nc.sbuf_tensor("AI", [128, 4096], I32))
    ps = stack.enter_context(nc.psum_tensor("ps", [128, 4096], F32))
    sems = {}
    for sname in ('dls', 'dlt', 'dlg', 'dvs', 'dvo', 'dxl', 'dws', 'dxq', 'dch', 'dul', 'dwd', 'dxr', 'dpl', 'dup', 'tt', 'tv', 'tc', 'tg', 'cc'):
        sems[sname] = stack.enter_context(nc.semaphore(sname))

    def bank(i, cols=512, off=0):
        assert off + cols <= 512
        return ps[:, i * 512 + off: i * 512 + off + cols]

    WQ, WK, WV = 0, 8192, 16384
    MSKI, MSK0 = 24576, 24640
    OB = 24832
    XNT = OB
    QT = OB + 8192
    KT = OB + 16384
    VE = OB + 25600
    VO = OB + 29760
    EXPT = OB + 31840
    ATT = OB + 35936
    XN = OB + 37984
    XN2 = OB + 40032
    XN2T = OB + 42080
    assert OB + 44128 <= 70656
    PWDQ = OB
    PWTS = OB + 6144
    XQ = OB
    WST = OB + 16384
    EVS = OB + 18432
    LD1 = OB + 26624
    SG2 = OB + 30720
    TO3 = OB + 34816
    SCR = OB + 38912
    XQS = OB + 40960
    WDT = OB
    UST = OB + 24576
    OBND = OB + 30720
    FFT = OB + 36864
    FFN_N = OB + 38912
    assert OB + 45056 <= 70656

    a16 = lambda off, n: A16[:, off: off + n]

    XR = 0
    X1B = 4096
    BN6 = 6144
    MVA, MVB = 6156, 6158
    MCOL, VCOL = 6160, 6161
    TMP0, TMP1, TMP2, TMP3, TMP4 = 6162, 6163, 6164, 6165, 6166
    SQC, RSC, NMR, EPSC, NEG8 = 6167, 6168, 6169, 6172, 6176
    RECIP = 6180
    SCAL = 6212
    MXA = 6240
    MXF = 6400
    SRED = 7424
    assert SRED + 512 <= 8192
    a32 = lambda off, n: A32[:, off: off + n]
    col = lambda off: A32[:, off: off + 1]
    ai = lambda off, n: AI[:, off: off + n]

    QSEM = {"sync": "dls", "scalar": "dlt", "gpsimd": "dlg"}

    def dma(q, out, in_, waits=(), sem=None):
        s = sem or QSEM[q]
        sch.op(q, lambda e, out=out, in_=in_: e.dma_start(out=out, in_=in_),
               waits=waits, inc=s, k=16)
        return C[s]

    def sdma(q, sem, out, in_, waits=()):
        return dma(q, out, in_, waits=tuple(waits) + ((sem, C[sem]),), sem=sem)

    def dmat(out, in_, waits=()):
        sch.op("scalar",
               lambda e, out=out, in_=in_: e.dma_start(out=out, in_=in_, transpose=True),
               waits=waits, inc="dlt", k=16)
        return C["dlt"]

    def vop(fn, waits=(), selfw=False):
        if selfw:
            waits = tuple(waits) + (("tv", C["tv"]),)
        sch.op("vector", fn, waits=waits, inc="tv")
        return C["tv"]

    def scop(fn, waits=(), selfw=False):
        if selfw:
            waits = tuple(waits) + (("tc", C["tc"]),)
        sch.op("scalar", fn, waits=waits, inc="tc")
        return C["tc"]

    def top(fn, waits=(), inc=None):
        sch.op("tensor", fn, waits=waits, inc=inc)
        return C["tt"]

    def ts(out, in0, s1, s2, op0, op1=None, accum=None):
        def f(e, out=out, in0=in0, s1=s1, s2=s2, op0=op0, op1=op1, accum=accum):
            kw = {}
            if op1 is not None:
                kw["op1"] = op1
            if accum is not None:
                kw["accum_out"] = accum
            return e.tensor_scalar(out=out, in0=in0, scalar1=s1, scalar2=s2,
                                   op0=op0, **kw)
        return f

    vop(lambda e: e.memset(a16(OB, 70656 - OB), 0.0))
    vop(lambda e: e.memset(a32(0, 8192), 0.0))
    vop(lambda e: e.memset(col(EPSC), EPS))
    CLR = vop(lambda e: e.memset(col(NEG8), -8.0))
    dma("sync", a16(MSKI, CHK), mski_d[:, :])
    MLD = dma("sync", a16(MSK0, CHK), msk0_d[:, :])
    WQKV_LD = 0
    for w_off, w_src in ((WQ, wq_d), (WK, wk_d), (WV, wv_d)):
        for icc in range(8):
            WQKV_LD = dma("gpsimd", a16(w_off + icc * 1024, 1024),
                          w_src[icc * 128:(icc + 1) * 128, :],
                          waits=(("tv", CLR),))

    prev_prep = {}

    def prep_weight(w_src, n_rows, rc, wT_dst):
        nblk = rc // 32
        nic = rc // 128
        for r in range(n_rows):
            buf = r % 2
            f32o = 0 if buf == 0 else 3072
            LD = sdma("sync", "dpl", a32(f32o, rc), w_src[r * 128:(r + 1) * 128, :],
                      waits=(("tv", prev_prep.get(("v", buf), CLR)),))
            wfp = a32(f32o, rc)
            bm, sbv, rb = a32(SRED, nblk), a32(SRED + 96, nblk), a32(SRED + 192, nblk)
            vop(lambda e, wfp=wfp, bm=bm: e.tensor_reduce(
                out=bm.rearrange("p (b o) -> p b o", o=1),
                in_=wfp.rearrange("p (b o) -> p b o", o=32),
                axis=AX.X, op=OP.max, apply_absolute_value=True),
                waits=(("dpl", LD),))
            vop(ts(sbv, bm, 1.0 / 7.0, 1e-12, OP.mult, OP.max), selfw=True)
            RBD = vop(lambda e, sbv=sbv, rb=rb: e.reciprocal(rb, sbv), selfw=True)
            y = ai(0, rc)
            wdq = a16(PWDQ + buf * 3072, rc)
            for b in range(nblk):
                vop(ts(y[:, b * 32:(b + 1) * 32], wfp[:, b * 32:(b + 1) * 32],
                       rb[:, b:b + 1], None, OP.mult),
                    waits=((("tv", RBD),) if b == 0 else ()))
            for b in range(nblk):
                vop(ts(wdq[:, b * 32:(b + 1) * 32], y[:, b * 32:(b + 1) * 32],
                       sbv[:, b:b + 1], None, OP.mult),
                    waits=((("dlt", prev_prep.get(("t", buf), 0)), ("tv", RBD)) if b == 0 else ()))
            VD = C["tv"]
            prev_prep[("v", buf)] = VD
            wts = a16(PWTS + buf * 3072, rc)
            TRD = 0
            for icc in range(nic):
                TRD = dmat(wts[:, icc * 128:(icc + 1) * 128]
                           .rearrange("p (a b) -> p a b", a=1),
                           wdq[:, icc * 128:(icc + 1) * 128],
                           waits=(("tv", VD), ("dlg", prev_prep.get(("g", buf), 0))))
            GD = dma("gpsimd",
                     wT_dst[r * nic * 128:(r + 1) * nic * 128, :]
                     .rearrange("(i p) f -> p i f", p=128),
                     wts.rearrange("p (i f) -> p i f", f=128),
                     waits=(("dlt", TRD),))
            prev_prep[("g", buf)] = GD
            prev_prep[("t", buf)] = TRD

    if upto not in ("none",):
        prep_weight(wg_d, 24, DM, wgT_sp)
        prep_weight(wu_d, 24, DM, wuT_sp)
        prep_weight(wd_d, 8, I3, wdT_sp)
    PREP_V = C["tv"]
    PREP_G = C["dlg"]
    RUN_ATTN = upto not in ("none", "prep")

    def kt_slot(hp, tbp):
        return KT + hp * 1152 + (64 if tbp == 0 else 640)

    def kt_halo_dst(hp, tbp):
        return KT + hp * 1152 + (576 if tbp == 0 else 0)

    def ln_norm(src_ap, dst_ap, extra_wait=(), dst_waits=()):
        vop(lambda e, src_ap=src_ap: e.bn_stats(out=a32(BN6, 6), in_=src_ap[:, 0:512]),
            waits=extra_wait)
        vop(lambda e, src_ap=src_ap: e.bn_stats(out=a32(BN6 + 6, 6), in_=src_ap[:, 512:1024]))
        vop(lambda e: e.bn_aggr(out=a32(MVA, 2), in_=a32(BN6, 6)), selfw=True)
        vop(lambda e: e.bn_aggr(out=a32(MVB, 2), in_=a32(BN6 + 6, 6)), selfw=True)
        vop(lambda e: e.tensor_tensor(out=col(TMP0), in0=col(MVA), in1=col(MVB), op=OP.add), selfw=True)
        vop(lambda e: e.tensor_tensor(out=col(TMP1), in0=col(MVA), in1=col(MVB), op=OP.subtract), selfw=True)
        vop(lambda e: e.tensor_tensor(out=col(TMP2), in0=col(MVA + 1), in1=col(MVB + 1), op=OP.add), selfw=True)
        vop(lambda e: e.tensor_tensor(out=col(TMP3), in0=col(TMP1), in1=col(TMP1), op=OP.mult), selfw=True)
        vop(ts(col(MCOL), col(TMP0), 0.5, None, OP.mult), selfw=True)
        vop(ts(col(TMP4), col(TMP3), 0.25, None, OP.mult), selfw=True)
        VST = vop(lambda e: e.scalar_tensor_tensor(
            out=col(VCOL), in0=col(TMP2), scalar=0.5, in1=col(TMP4),
            op0=OP.mult, op1=OP.add), selfw=True)
        SCQ = scop(lambda e: e.activation(out=col(SQC), in_=col(VCOL), func=AF.Sqrt,
                                          bias=col(EPSC), scale=1.0),
                   waits=(("tv", VST),))
        vop(lambda e: e.reciprocal(col(RSC), col(SQC)), waits=(("tc", SCQ),))
        VRC = vop(lambda e: e.scalar_tensor_tensor(
            out=col(NMR), in0=col(MCOL), scalar=-1.0, in1=col(RSC),
            op0=OP.mult, op1=OP.mult), selfw=True)
        return scop(lambda e, src_ap=src_ap, dst_ap=dst_ap: e.activation(
            out=dst_ap, in_=src_ap, func=AF.Identity, bias=col(NMR), scale=col(RSC)),
            waits=(("tv", VRC),) + tuple(dst_waits))

    X1D = {}
    X1SP = {}
    VSPW = {}
    DBGD = {}
    PV_of = {}
    DMAT_XN = {}
    T2D_of = {}
    XSPILL = {}
    TBMM = {}
    prev_exp = 0
    prev_attev = (0, 0)
    prev_pj = {}

    for tb in range(NTB if RUN_ATTN else 0):
        tbp = tb % 2
        TRD_tb = 0
        for j in range(4):
            tau = tb * 4 + j
            xs = XR + (tau % 4) * 1024
            XL = sdma("sync", "dxl", a32(xs, 1024), x_d[tau * 128:(tau + 1) * 128, :],
                      waits=(("tv", max(PREP_V, X1D.get(tau - 4, 0))),))
            xn = a16(XN + (tau % 2) * 1024, 1024)
            XND = ln_norm(a32(xs, 1024), xn, extra_wait=(("dxl", XL),),
                          dst_waits=(("dlt", DMAT_XN.get(tau - 2, 0)),
                                     ("dlg", DBGD.get(("xn", tau - 2), 0))))
            sl = XNT + (tau % 8) * 1024
            TRD_tb = dmat(a16(sl, 1024).rearrange("p (c f) -> p c f", f=128),
                          xn, waits=(("tc", XND), ("dlg", PREP_G),
                                     ("tt", TBMM.get(tb - 2, 0))))
            DMAT_XN[tau] = TRD_tb
            if "xn" in sbuf_dbg:
                DBGD[("xn", tau)] = dma("gpsimd",
                    sbuf_dbg["xn"][tau * 128:(tau + 1) * 128, :], xn,
                    waits=(("tc", XND),))
        stage = XNT + (4 * tbp) * 1024

        def rhs_q(i, stage=stage):
            return (a16(stage, 4096).rearrange("p (t f) -> p t f", f=1024)
                    [:, :, i * 128:(i + 1) * 128])

        for which, woff in (("q", WQ), ("k", WK)):
            for oc in range(8):
                bk = oc % 2
                w0 = [("dlt", TRD_tb), ("dlg", WQKV_LD),
                      ("tv", max(prev_pj.get(bk, 0), prev_attev[0])),
                      ("tc", prev_attev[1]),
                      ("dlg", DBGD.get(("pj", tb - 2), 0))]
                MMD = 0
                for i in range(8):
                    MMD = top(lambda e, bk=bk, woff=woff, oc=oc, i=i, stage=stage: e.matmul(
                        bank(bk),
                        a16(woff + i * 1024 + oc * 128, 128),
                        (a16(stage, 4096).rearrange("p (t f) -> p t f", f=1024)
                         [:, :, i * 128:(i + 1) * 128]),
                        start=(i == 0), stop=(i == 7)),
                        waits=(w0 if i == 0 else ()),
                        inc=("tt" if i == 7 else None))
                if which == "q":
                    EV = vop(ts(a16(QT + oc * 1024 + tbp * 512, 512), bank(bk),
                                1.0, None, OP.mult), waits=(("tt", MMD),))
                    if "qt" in sbuf_dbg:
                        DBGD[("pj", tb)] = dma("gpsimd",
                            sbuf_dbg["qt"][oc * 128:(oc + 1) * 128,
                                           tb * 512:(tb + 1) * 512],
                            a16(QT + oc * 1024 + tbp * 512, 512), waits=(("tv", EV),))
                else:
                    vop(ts(a16(kt_slot(oc, tbp), 512), bank(bk), 0.125, None, OP.mult),
                        waits=(("tt", MMD),))
                    EV = vop(ts(a16(kt_halo_dst(oc, tbp), 64), bank(bk, 64, 448),
                                0.125, None, OP.mult))
                    if "kt" in sbuf_dbg:
                        DBGD[("pj", tb)] = dma("gpsimd",
                            sbuf_dbg["kt"][oc * 128:(oc + 1) * 128,
                                           tb * 512:(tb + 1) * 512],
                            a16(kt_slot(oc, tbp), 512), waits=(("tv", EV),))
                prev_pj[bk] = EV
        for j in range(4):
            tau = tb * 4 + j
            for half in range(2):
                bk = half
                w0 = [("tv", max(prev_pj.get(bk, 0), prev_attev[0])),
                      ("tc", prev_attev[1]), ("tt", PV_of.get(tau - 4, 0))]
                if tau - 4 in VSPW:
                    w0.append(("dvs", VSPW[tau - 4]))
                MMD = 0
                for i in range(8):
                    MMD = top(lambda e, bk=bk, i=i, half=half, stage=stage, j=j: e.matmul(
                        bank(bk), a16(stage + j * 1024 + i * 128, 128),
                        a16(WV + i * 1024 + half * 512, 512),
                        start=(i == 0), stop=(i == 7)),
                        waits=(w0 if i == 0 else ()),
                        inc=("tt" if i == 7 else None))
                ve = VE + (tau % 4) * 1040
                vdst = (a16(ve, 1040).rearrange("p (h f) -> p h f", f=65)
                        [:, half * 8:(half + 1) * 8, 0:64])
                EV = vop(ts(vdst, bank(bk), 1.0, None, OP.mult), waits=(("tt", MMD),))
                prev_pj[bk] = EV
            ve = VE + (tau % 4) * 1040
            VED = vop(lambda e, ve=ve: e.memset(
                a16(ve, 1040).rearrange("p (h f) -> p h f", f=65)[:, :, 64:65], 1.0))
            VSPW[tau] = dma("gpsimd", v_sp[tau * 128:(tau + 1) * 128, :], a16(ve, 1040),
                            waits=(("tv", VED), ("dvs", C["dvs"])), sem="dvs")
        QKREADY = C["tv"]
        TBMM[tb] = C["tt"]
        if upto == "proj":
            continue

        for j in range(4):
            tau = tb * 4 + j
            pslot = tau % 2
            base = kt_slot(0, tbp) - KT
            vo = VO + (tau % 2) * 1040
            if tau == 0:
                vodl = sdma("sync", "dvo", A16[64:128, vo:vo + 1040], v_sp[0:64, :],
                            waits=(("dvs", VSPW[tau]), ("tv", CLR)))
            else:
                vodl = sdma("sync", "dvo", a16(vo, 1040),
                            v_sp[tau * 128 - 64: tau * 128 + 64, :],
                            waits=(("dvs", VSPW[tau]), ("tt", PV_of.get(tau - 2, 0))))
            first_qk = True
            QKD = 0
            for mloc, bnkE, bnkO in ((0, 2, 4), (1, 3, 5)):
                wcol = base + j * 128 + mloc * 64 - 64
                for hp in range(8):
                    for par, bnk in ((0, bnkE), (1, bnkO)):
                        w = (("tv", QKREADY), ("tc", prev_exp)) if first_qk else ()
                        first_qk = False
                        QKD = top(lambda e, hp=hp, wcol=wcol, bnk=bnk, j=j,
                                  tbp=tbp, par=par, mloc=mloc: e.matmul(
                            bank(bnk, 64, hp * 64),
                            A16[par * 64: par * 64 + 64,
                                KT + hp * 1152 + wcol: KT + hp * 1152 + wcol + 128],
                            A16[par * 64: par * 64 + 64,
                                QT + hp * 1024 + tbp * 512 + j * 128 + mloc * 64:
                                QT + hp * 1024 + tbp * 512 + j * 128 + mloc * 64 + 64],
                            start=True, stop=True, tile_position=(par * 64, 0)),
                            waits=w, inc=("tt" if hp == 7 else None))
            EXD = 0
            for qi, bnk in enumerate((2, 4, 3, 5)):
                EXD = scop(lambda e, qi=qi, bnk=bnk, pslot=pslot: e.activation(
                    out=a16(EXPT + pslot * 2048 + qi * 512, 512), in_=bank(bnk),
                    func=AF.Exp, bias=col(NEG8), scale=1.0),
                    waits=((("tt", QKD), ("dls", MLD),
                            ("dlg", DBGD.get(("ex", tau - 2), 0))) if qi == 0 else ()))
            prev_exp = EXD
            MKD = 0
            for half in range(2):
                moff = (MSK0 if (tau == 0 and half == 0) else MSKI)
                eap = (a16(EXPT + pslot * 2048 + half * 1024, 1024)
                       .rearrange("p (t f) -> p t f", f=64))
                msk = (a16(moff, 64).rearrange("p (o f) -> p o f", o=1)
                       .broadcast_to((128, 16, 64)))
                MKD = vop(lambda e, eap=eap, msk=msk: e.tensor_tensor(
                    out=eap, in0=eap, in1=msk, op=OP.mult),
                    waits=((("tc", EXD),) if half == 0 else ()))

            def att_slot(h):
                return (6, 65 * h) if h < 7 else ((7, 65 * (h - 7)) if h < 14
                                                  else (0, 65 * (h - 14)))
            if upto == "qkexp":
                prev_attev = (MKD, EXD)
                PV_of[tau] = QKD
                continue
            first_pv = True
            PVD = 0
            for h in range(H):
                hp, par = h // 2, h % 2
                for mloc in range(2):
                    qi = {(0, 0): 0, (0, 1): 1, (1, 0): 2, (1, 1): 3}[(mloc, par)]
                    ecol = EXPT + pslot * 2048 + qi * 512 + hp * 64
                    vtile = (VO + (tau % 2) * 1040) if mloc == 0 else (VE + (tau % 4) * 1040)
                    bnk, soff = att_slot(h)
                    w = ()
                    if first_pv:
                        w = [("tv", MKD), ("tv", prev_attev[0]), ("tc", prev_attev[1]),
                             ("dvo", vodl), ("dlg", DBGD.get(("at", tau - 2), 0))]
                        first_pv = False
                    PVD = top(lambda e, ecol=ecol, vtile=vtile, h=h, bnk=bnk,
                              soff=soff, mloc=mloc: e.matmul(
                        ps[64 * mloc: 64 * mloc + 64,
                           bnk * 512 + soff: bnk * 512 + soff + 65],
                        a16(ecol, 64), a16(vtile + h * 65, 65),
                        start=True, stop=True, tile_position=(0, 64 * mloc)),
                        waits=w, inc=("tt" if (h == H - 1 and mloc == 1) else None))
            PV_of[tau] = PVD
            if upto == "pv":
                prev_attev = (C["tv"], C["tc"])
                continue
            rc = RECIP + pslot * 16
            RCD = 0
            for gi, (bnk, nsl, rbase) in enumerate(((6, 7, 0), (7, 7, 7), (0, 2, 14))):
                RCD = vop(lambda e, bnk=bnk, nsl=nsl, rbase=rbase, rc=rc: e.reciprocal(
                    a32(rc + rbase, nsl),
                    ps[:, bnk * 512 + 64: bnk * 512 + 64 + (nsl - 1) * 65 + 1: 65]),
                    waits=((("tt", PVD),) if gi == 0 else ()))
            attb = ATT + pslot * 1024
            AEV, AEC = 0, 0
            for h in list(range(0, 7)) + [14, 15]:
                bnk, soff = att_slot(h)
                AEV = vop(ts(a16(attb + h * 64, 64),
                             ps[:, bnk * 512 + soff: bnk * 512 + soff + 64],
                             a32(rc + h, 1), None, OP.mult),
                          waits=((("tv", RCD),) if h == 0 else ()))
            for h in range(7, 14):
                bnk, soff = att_slot(h)
                AEC = scop(lambda e, h=h, bnk=bnk, soff=soff, attb=attb, rc=rc: e.activation(
                    out=a16(attb + h * 64, 64),
                    in_=ps[:, bnk * 512 + soff: bnk * 512 + soff + 64],
                    func=AF.Identity, bias=0.0, scale=a32(rc + h, 1)),
                    waits=(("tv", RCD),))
            prev_attev = (AEV, AEC)
            if upto == "evac":
                continue
            if "att" in sbuf_dbg:
                DBGD[("at", tau)] = dma("gpsimd",
                    sbuf_dbg["att"][tau * 128:(tau + 1) * 128, :],
                    a16(attb, 1024), waits=(("tv", AEV), ("tc", AEC)))
            if "expt" in sbuf_dbg:
                DBGD[("ex", tau)] = dma("gpsimd",
                    sbuf_dbg["expt"][tau * 16 * 128: (tau + 1) * 16 * 128, :]
                    .rearrange("(a p) f -> p a f", p=128),
                    a16(EXPT + pslot * 2048, 2048).rearrange("p (a f) -> p a f", f=128),
                    waits=(("tv", MKD),))
            xs = XR + (tau % 4) * 1024
            attn = a16(XN + (tau % 2) * 1024, 1024)
            AND = ln_norm(a16(attb, 1024), attn, extra_wait=(("tc", AEC), ("tv", AEV)),
                          dst_waits=(("dlg", DBGD.get(("xn", tau), 0)),))
            x1b = X1B + (tau % 2) * 1024
            X1V = vop(lambda e, x1b=x1b, xs=xs, attn=attn: e.tensor_tensor(
                out=a32(x1b, 1024), in0=a32(xs, 1024), in1=attn, op=OP.add),
                waits=(("tc", AND), ("dlg", X1SP.get(tau - 2, 0))))
            X1D[tau] = X1V
            X1SP[tau] = dma("gpsimd", x1_sp[tau * 128:(tau + 1) * 128, :], a32(x1b, 1024),
                            waits=(("tv", X1V),))
            xn2 = a16(XN2 + (tau % 2) * 1024, 1024)
            XN2D = ln_norm(a32(x1b, 1024), xn2,
                           dst_waits=(("dlt", T2D_of.get(tau - 2, 0)),))
            scr_a = a16(XN + ((tau + 1) % 2) * 1024, 1024)
            vop(ts(scr_a, xn2, 1.0, None, OP.mult, OP.max, a32(MXA + 2 * (tau % 64), 1)),
                waits=(("tc", XN2D),))
            MXD = vop(ts(scr_a, xn2, -1.0, None, OP.mult, OP.max,
                         a32(MXA + 2 * (tau % 64) + 1, 1)))
            x2t = a16(XN2T + (tau % 2) * 1024, 1024)
            T2D = dmat(x2t.rearrange("p (c f) -> p c f", f=128), xn2,
                       waits=(("tc", XN2D), ("dlg", XSPILL.get(tau - 2, 0))))
            T2D_of[tau] = T2D
            XSPILL[tau] = dma("gpsimd",
                xn2t_sp[:, tau * 128:(tau + 1) * 128].rearrange("(c p) f -> p c f", p=128),
                x2t.rearrange("p (c f) -> p c f", f=128),
                waits=(("dlt", T2D), ("tv", MXD)))
    ATTV, ATTC, ATTG = C["tv"], C["tc"], C["dlg"]
    RUN_AR1 = RUN_ATTN and upto not in ("attn", "proj", "qkexp", "pv", "evac")

    sc_slot = [0]

    def fold_reduce(src_off, nsl, out_off, waits):
        FV = vop(ts(a16(SCR, nsl), a32(src_off, nsl), 1.0, None, OP.mult, OP.max,
                    col(SRED + 256)), waits=waits)
        RG = dma("scalar", red_sp[0:1, :], col(SRED + 256), waits=(("tv", FV),))
        RL = dma("sync", A32[0:1, SRED: SRED + 128], red_sp[0:1, :],
                 waits=(("dlt", RG),))
        return vop(lambda e, out_off=out_off: e.tensor_reduce(
            out=A32[0:1, out_off: out_off + 1], in_=A32[0:1, SRED: SRED + 128],
            axis=AX.X, op=OP.max, apply_absolute_value=False),
            waits=(("dls", RL),), selfw=True)

    def collective(idx, slot_srcs, waits):
        VZ = vop(lambda e: e.memset(A32[0:1, SRED + 304: SRED + 320], 0.0), waits=waits)
        for slot, off in slot_srcs:
            VZ = vop(lambda e, slot=slot, off=off: e.tensor_copy(
                A32[0:1, SRED + 304 + slot: SRED + 305 + slot],
                A32[0:1, off: off + 1]), selfw=True)
        W = dma("scalar", ar_in[idx][:, :], A32[0:1, SRED + 304: SRED + 320],
                waits=(("tv", VZ),))
        sch.op("gpsimd", lambda e, idx=idx: e.collective_compute(
            "AllReduce", OP.max, replica_groups=[list(range(ncores))],
            ins=[ar_in[idx].ap().opt()], outs=[ar_out[idx].ap().opt()]),
            waits=(("dlt", W),), inc="cc", k=1)
        return dma("sync", A32[0:1, SRED + 352: SRED + 368], ar_out[idx][:, :],
                   waits=(("cc", C["cc"]),))

    def bcast(dst_col, src_off, waits):
        slt = sc_slot[0] % 32
        sc_slot[0] += 1
        W = dma("scalar", sc_sp[0:1, slt:slt + 1], A32[0:1, src_off: src_off + 1],
                waits=waits)
        return dma("scalar", col(dst_col), sc_sp[0:1, slt:slt + 1].partition_broadcast(128),
                   waits=(("dlt", W),))

    if not RUN_AR1:
        _dbg_copy(dbg_outs, dma, a16, a32, OB, C, F32)
        sch.op("gpsimd", None, waits=(("dlg", C["dlg"]),))
        return _finish(nc, sch, sems, stack)
    FR = fold_reduce(MXA, 128, SRED + 300,
                     waits=(("tv", ATTV), ("tc", ATTC), ("dlg", ATTG)))
    ARL = collective(0, [(0, SRED + 300)], waits=(("tv", FR),))
    vop(ts(A32[0:1, SRED + 400: SRED + 401], A32[0:1, SRED + 352: SRED + 353],
           1.0 / QMAX, 1e-12, OP.mult, OP.max), waits=(("dls", ARL),))
    SCV = vop(lambda e: e.reciprocal(A32[0:1, SRED + 401: SRED + 402],
                                     A32[0:1, SRED + 400: SRED + 401]), selfw=True)
    bcast(SCAL + 1, SRED + 400, waits=(("tv", SCV),))
    L_rs1 = bcast(SCAL + 0, SRED + 401, waits=(("tv", SCV),))

    prev_f = {}

    def ffn_pass_gen(wT, spill, ready_dls):
        mxn = 0
        prev_sb_mm = C["tt"]
        for sbi in range(NSB):
            XQD = 0
            for dc in range(8):
                buf = dc % 2
                LD = sdma("sync", "dxq", a16(XQS + buf * 2048, SBT),
                          xn2t_sp[dc * 128:(dc + 1) * 128, sbi * SBT:(sbi + 1) * SBT],
                          waits=(("tv", prev_f.get(("ld", buf), 0)),
                                 ("dlg", ATTG), ("tv", ATTV), ("tc", ATTC)))
                vop(ts(ai(0, SBT), a16(XQS + buf * 2048, SBT), col(SCAL + 0), QMAX,
                       OP.mult, OP.min), waits=(("dxq", LD), ("dlt", ready_dls)))
                XQD = vop(ts(a16(XQ + dc * SBT, SBT), ai(0, SBT), 1.0, None, OP.mult),
                          waits=(("tt", prev_sb_mm),
                                 ("dlg", DBGD.get(("xq", dc, sbi - 1), 0))), selfw=True)
                prev_f[("ld", buf)] = XQD
                if "xq" in sbuf_dbg:
                    DBGD[("xq", dc, sbi)] = dma("gpsimd",
                        sbuf_dbg["xq"][dc * 128:(dc + 1) * 128, sbi * SBT:(sbi + 1) * SBT],
                        a16(XQ + dc * SBT, SBT), waits=(("tv", XQD),))
            for oc in range(24):
                wb = oc % 2
                WL = sdma("sync", "dws",
                          a16(WST + wb * 1024, 1024).rearrange("p (i f) -> p i f", f=128),
                          wT[oc * 1024:(oc + 1) * 1024, :].rearrange("(i p) f -> p i f", p=128),
                          waits=(("tt", prev_f.get(("wst", wb), 0)),))
                for tb4 in range(SBT // 512):
                    bk = (oc % 2) * 4 + tb4
                    w0 = [("dws", WL), ("tv", max(XQD, prev_f.get(("fbv", bk), 0))),
                          ("tc", prev_f.get(("fbc", bk), 0))]
                    MMD = 0
                    for i in range(8):
                        MMD = top(lambda e, bk=bk, wb=wb, i=i, tb4=tb4: e.matmul(
                            bank(bk), a16(WST + wb * 1024 + i * 128, 128),
                            a16(XQ + i * SBT + tb4 * 512, 512),
                            start=(i == 0), stop=(i == 7)),
                            waits=(w0 if i == 0 else ()),
                            inc=("tt" if i == 7 else None))
                    if tb4 == SBT // 512 - 1:
                        prev_f[("wst", wb)] = MMD
                    prev_sb_mm = MMD
                    ev = EVS + (oc % 4) * 2048 + tb4 * 512
                    ECD = scop(lambda e, ev=ev, bk=bk: e.activation(
                        out=a16(ev, 512), in_=bank(bk), func=AF.Identity, bias=0.0,
                        scale=col(SCAL + 1)),
                        waits=(("tt", MMD), ("dlg", prev_f.get(("sp", oc % 4, tb4), 0))))
                    prev_f[("fbc", bk)] = ECD
                    slot = MXF + (mxn % 768)
                    mxn += 1
                    MXD = vop(lambda e, ev=ev, slot=slot: e.tensor_reduce(
                        out=a32(slot, 1), in_=a16(ev, 512), axis=AX.X, op=OP.max,
                        apply_absolute_value=True), waits=(("tc", ECD),))
                    prev_f[("fbv", bk)] = MXD
                    SPD = dma("gpsimd",
                              spill[oc * 128:(oc + 1) * 128,
                                    sbi * SBT + tb4 * 512: sbi * SBT + (tb4 + 1) * 512],
                              a16(ev, 512), waits=(("tv", MXD),))
                    prev_f[("sp", oc % 4, tb4)] = SPD
                yield
        yield

    def ffn_pass(wT, spill, ready_dls, inter=None):
        g = ffn_pass_gen(wT, spill, ready_dls)
        for _ in g:
            if inter is not None:
                try:
                    next(inter)
                except StopIteration:
                    inter = None
        if inter is not None:
            for _ in inter:
                pass
        return C["tv"], C["dlg"]

    MXCLR = vop(lambda e: e.memset(a32(MXF, 1024), 0.0), waits=(("tv", ATTV),))
    if upto == "ar1":
        sch.op("gpsimd", None, waits=(("dlg", C["dlg"]), ("dls", C["dls"]), ("tv", C["tv"])))
        return _finish(nc, sch, sems, stack)
    GV, GG = ffn_pass(wgT_sp, gt_sp, L_rs1)
    if upto == "gate":
        _dbg_copy(dbg_outs, dma, a16, a32, OB, C, F32)
        sch.op("gpsimd", None, waits=(("dlg", C["dlg"]), ("dls", C["dls"]), ("tv", C["tv"]), ("tc", C["tc"])))
        return _finish(nc, sch, sems, stack)
    FR = fold_reduce(MXF, 768, SRED + 300, waits=(("tv", GV),))
    ARL2 = collective(1, [(0, SRED + 300)], waits=(("tv", FR), ("dlg", GG)))
    vop(ts(A32[0:1, SRED + 404: SRED + 405], A32[0:1, SRED + 352: SRED + 353],
           1.0 / QMAX, 1e-12, OP.mult, OP.max), waits=(("dls", ARL2),))
    vop(lambda e: e.reciprocal(A32[0:1, SRED + 405: SRED + 406],
                               A32[0:1, SRED + 404: SRED + 405]), selfw=True)
    SCV = vop(ts(A32[0:1, SRED + 406: SRED + 407], A32[0:1, SRED + 404: SRED + 405],
                 1.0 / 65536.0, None, OP.mult), selfw=True)
    bcast(SCAL + 2, SRED + 405, waits=(("tv", SCV),))
    bcast(SCAL + 3, SRED + 404, waits=(("tv", SCV),))
    L_s3 = bcast(SCAL + 4, SRED + 406, waits=(("tv", SCV),))
    MXCLR2 = vop(lambda e: e.memset(a32(MXF, 768), 0.0), waits=(("tv", GV),))

    def p1_gen():
        pc = {}
        for ct in range(24 * NCB):
            dc, cb = ct // NCB, ct % NCB
            buf = ct % 2
            gsl = a16(LD1 + buf * 2048, CW)
            LD = sdma("sync", "dch", gsl,
                      gt_sp[dc * 128:(dc + 1) * 128, cb * CW:(cb + 1) * CW],
                      waits=(("tc", pc.get(("y1", buf), 0)), ("dlt", L_s3)))
            Y1D = scop(lambda e, gsl=gsl: e.activation(
                out=ai(2048, CW), in_=gsl, func=AF.Identity, bias=0.0,
                scale=col(SCAL + 2)),
                waits=(("dch", LD), ("tv", pc.get("stt", 0))))
            pc[("y1", buf)] = Y1D
            sg = a16(SG2 + buf * 2048, CW)
            SGD = scop(lambda e, sg=sg: e.activation(
                out=sg, in_=ai(2048, CW), func=AF.Sigmoid, bias=0.0,
                scale=col(SCAL + 3)),
                waits=(("tv", pc.get(("ysq", buf), 0)),), selfw=True)
            YG1F = scop(lambda e: e.activation(
                out=a32(0, CW), in_=ai(2048, CW), func=AF.Identity, bias=0.0,
                scale=col(SCAL + 4)),
                waits=(("tv", pc.get("stt", ATTV)),), selfw=True)
            YSQ = vop(ts(ai(2048, CW), sg, 65536.0, 65535.0, OP.mult, OP.min),
                      waits=(("tc", YG1F),))
            pc[("ysq", buf)] = YSQ
            tt_ = a16(TO3 + buf * 2048, CW)
            vop(lambda e, tt_=tt_: e.scalar_tensor_tensor(
                out=tt_, in0=a32(0, CW), scalar=1.0, in1=ai(2048, CW),
                op0=OP.mult, op1=OP.mult),
                waits=(("dlg", pc.get(("sp", buf), 0)),), selfw=True)
            pc["stt"] = C["tv"]
            TTD = vop(lambda e, tt_=tt_, ct=ct: e.tensor_reduce(
                out=a32(MXF + 768 + ct, 1), in_=tt_, axis=AX.X, op=OP.max,
                apply_absolute_value=True), selfw=True)
            pc[("sp", buf)] = dma("gpsimd",
                t_sp[dc * 128:(dc + 1) * 128, cb * CW:(cb + 1) * CW], tt_,
                waits=(("tv", TTD),))
            yield

    UV, UG = ffn_pass(wuT_sp, up_sp, L_rs1, inter=p1_gen())
    if upto == "up":
        _dbg_copy(dbg_outs, dma, a16, a32, OB, C, F32)
        sch.op("gpsimd", None, waits=(("dlg", C["dlg"]), ("dls", C["dls"]), ("tv", C["tv"]), ("tc", C["tc"])))
        return _finish(nc, sch, sems, stack)

    P1V, P1G, P1C = C["tv"], C["dlg"], C["tc"]
    if upto == "p1":
        _dbg_copy(dbg_outs, dma, a16, a32, OB, C, F32)
        sch.op("gpsimd", None, waits=(("dlg", C["dlg"]), ("dls", C["dls"]), ("tv", C["tv"]), ("tc", C["tc"])))
        return _finish(nc, sch, sems, stack)

    FRa = fold_reduce(MXF, 768, SRED + 300, waits=(("tv", max(UV, P1V)),))
    FRb = fold_reduce(MXF + 768, 256, SRED + 301, waits=(("tv", FRa),))
    ARL3 = collective(2, [(0, SRED + 300), (1, SRED + 301)],
                      waits=(("tv", FRb), ("dlg", max(UG, P1G))))
    vop(ts(A32[0:1, SRED + 408: SRED + 409], A32[0:1, SRED + 352: SRED + 353],
           1.0 / QMAX, 1e-12, OP.mult, OP.max), waits=(("dls", ARL3),))
    vop(lambda e: e.reciprocal(A32[0:1, SRED + 409: SRED + 410],
                               A32[0:1, SRED + 408: SRED + 409]), selfw=True)
    vop(ts(A32[0:1, SRED + 410: SRED + 411], A32[0:1, SRED + 353: SRED + 354],
           1.0 / QMAX, 1e-12, OP.mult, OP.max))
    vop(lambda e: e.reciprocal(A32[0:1, SRED + 411: SRED + 412],
                               A32[0:1, SRED + 410: SRED + 411]), selfw=True)
    SCV = vop(lambda e: e.tensor_tensor(
        out=A32[0:1, SRED + 412: SRED + 413], in0=A32[0:1, SRED + 408: SRED + 409],
        in1=A32[0:1, SRED + 410: SRED + 411], op=OP.mult), selfw=True)
    bcast(SCAL + 5, SRED + 409, waits=(("tv", SCV),))
    bcast(SCAL + 6, SRED + 411, waits=(("tv", SCV),))
    L_p2 = bcast(SCAL + 7, SRED + 412, waits=(("tv", SCV),))
    MXCLR3 = vop(lambda e: e.memset(a32(MXF, 1024), 0.0), waits=(("tv", max(UV, P1V)),))

    prev_c = {}
    HW_ = min(1024, CW)
    NH = CW // HW_
    for ct in range(24 * NCB):
        dc, cb = ct // NCB, ct % NCB
        b4 = ct % 4
        tsl = a16(XQ + b4 * 2048, CW)
        usl = a16(WST + b4 * 2048, CW)
        uu = a16(LD1 + b4 * 2048, CW)
        LT = sdma("sync", "dch", tsl,
                  t_sp[dc * 128:(dc + 1) * 128, cb * CW:(cb + 1) * CW],
                  waits=(("tc", prev_c.get(("y2", b4), P1C)), ("dlt", L_p2)))
        LU = sdma("scalar", "dup", usl,
                  up_sp[dc * 128:(dc + 1) * 128, cb * CW:(cb + 1) * CW],
                  waits=(("tc", prev_c.get(("yu", b4), P1C)),
                         ("tv", P1V if ct < 4 else 0),
                         ("dlg", P1G if ct < 4 else 0)))
        Y2D = YUD = 0
        for h in range(NH):
            qa = (ct * NH + h) % 2
            Y2D = scop(lambda e, tsl=tsl, h=h, qa=qa: e.activation(
                out=ai(qa * 1024, HW_), in_=tsl[:, h * HW_:(h + 1) * HW_],
                func=AF.Identity, bias=0.0, scale=col(SCAL + 6)),
                waits=(("dch", LT), prev_c.get(("sttq", qa), ("tv", P1V))))
            YUD = scop(lambda e, usl=usl, h=h, qa=qa: e.activation(
                out=ai(2048 + qa * 1024, HW_), in_=usl[:, h * HW_:(h + 1) * HW_],
                func=AF.Identity, bias=0.0, scale=col(SCAL + 5)),
                waits=(("dup", LU),), selfw=True)
            STT = vop(lambda e, uu=uu, h=h, qa=qa: e.scalar_tensor_tensor(
                out=uu[:, h * HW_:(h + 1) * HW_], in0=ai(qa * 1024, HW_),
                scalar=col(SCAL + 7), in1=ai(2048 + qa * 1024, HW_),
                op0=OP.mult, op1=OP.mult),
                waits=(("tc", YUD), prev_c.get(("sp", b4), ("dlg", P1G))),
                selfw=True)
            prev_c[("sttq", qa)] = ("tv", STT)
            RD = vop(lambda e, uu=uu, h=h, ct=ct: e.tensor_reduce(
                out=a32(MXF + (ct * NH + h) % 768, 1),
                in_=uu[:, h * HW_:(h + 1) * HW_], axis=AX.X, op=OP.max,
                apply_absolute_value=True), selfw=True)
            SPU = 0
            for k8 in range(h * (HW_ // 256), (h + 1) * (HW_ // 256)):
                tbk = cb * (CW // 256) + k8
                SPU = dma("gpsimd",
                          u_sp[tbk * I3 + dc * 128: tbk * I3 + (dc + 1) * 128, :],
                          uu[:, k8 * 256:(k8 + 1) * 256], waits=(("tv", RD),))
        prev_c[("y2", b4)] = Y2D
        prev_c[("yu", b4)] = YUD
        prev_c[("sp", b4)] = ("dlg", SPU)
    P2V, P2G, P2S = C["tv"], C["dlg"], C["dls"]
    if upto == "p2":
        _dbg_copy(dbg_outs, dma, a16, a32, OB, C, F32)
        sch.op("gpsimd", None, waits=(("dlg", C["dlg"]), ("dls", C["dls"]), ("tv", C["tv"]), ("tc", C["tc"])))
        return _finish(nc, sch, sems, stack)

    FR = fold_reduce(MXF, 768, SRED + 300, waits=(("tv", P2V),))
    ARL4 = collective(3, [(0, SRED + 300)],
                      waits=(("tv", FR), ("dlg", P2G), ("dls", P2S)))
    vop(ts(A32[0:1, SRED + 414: SRED + 415], A32[0:1, SRED + 352: SRED + 353],
           1.0 / QMAX, 1e-12, OP.mult, OP.max), waits=(("dls", ARL4),))
    SCV = vop(lambda e: e.reciprocal(A32[0:1, SRED + 415: SRED + 416],
                                     A32[0:1, SRED + 414: SRED + 415]), selfw=True)
    bcast(SCAL + 8, SRED + 415, waits=(("tv", SCV),))
    L_p8 = bcast(SCAL + 9, SRED + 414, waits=(("tv", SCV),))

    WDL = 0
    wdT_all = wdT_sp[:, :].rearrange("(jc icc p) f -> p jc icc f", p=128, icc=24)
    for icc in range(24):
        WDL = sdma("sync", "dwd",
                   a16(WDT + icc * 1024, 1024).rearrange("p (i f) -> p i f", f=128),
                   wdT_all[:, :, icc, :],
                   waits=(("tv", P2V), ("tc", C["tc"])))
    prev_p8 = {}
    NBLK = S // 256
    ODMA = 0
    TTF = C["tt"]
    uinfo = {}
    for b in range(NBLK + 1):
        if b < NBLK:
            us = 12288 + (b % 2) * 6144
            ob2 = (b % 2) * 6144
            ULD = sdma("sync", "dul",
                       a16(us, 6144).rearrange("p (i f) -> p i f", f=256),
                       u_sp[b * I3: (b + 1) * I3, :].rearrange("(i p) f -> p i f", p=128),
                       waits=(("tc", prev_p8.get(("ust", b % 2), 0)), ("dlt", L_p8)))
            OBD = 0
            for hh in range(2):
                YOD = scop(lambda e, hh=hh, us=us: e.activation(
                    out=ai(0, 3072), in_=a16(us + hh * 3072, 3072), func=AF.Identity,
                    bias=0.0, scale=col(SCAL + 8)),
                    waits=((("dul", ULD), ("tv", P2V if b == 0 else 0))
                           if hh == 0 else ()),
                    selfw=True)
                prev_p8[("ust", b % 2)] = YOD
                OBD = scop(lambda e, hh=hh, ob2=ob2: e.activation(
                    out=a16(ob2 + hh * 3072, 3072), in_=ai(0, 3072), func=AF.Identity,
                    bias=0.0, scale=1.0),
                    waits=((("tt", prev_p8.get(("mmb", b % 2), TTF)),)
                           if hh == 0 else ()),
                    selfw=True)
            uinfo[b] = (OBD, ob2)
        if b == 0:
            continue
        bb = b - 1
        OBDp, obp = uinfo.pop(bb)
        grp = (bb % 2) * 4
        MMD = 0
        for ti in range(2):
            tau = bb * 2 + ti
            for jh in range(2):
                bk = grp + ti * 2 + jh
                w0 = [("tc", OBDp), ("dwd", WDL),
                      ("tv", prev_p8.get(("ev", bk), 0)), ("tc", prev_p8.get(("ec", bk), 0))]
                MMD = 0
                for icc in range(24):
                    MMD = top(lambda e, bk=bk, icc=icc, ti=ti, jh=jh, obp=obp: e.matmul(
                        bank(bk),
                        a16(obp, 6144).rearrange("p (i f) -> p i f", f=256)
                        [:, icc, ti * 128:(ti + 1) * 128],
                        a16(WDT + icc * 1024 + jh * 512, 512),
                        start=(icc == 0), stop=(icc == 23)),
                        waits=(w0 if icc == 0 else ()),
                        inc=("tt" if icc == 23 else None))
                ECD = scop(lambda e, bk=bk, ti=ti, jh=jh: e.activation(
                    out=a16(FFT + ti * 1024 + jh * 512, 512), in_=bank(bk),
                    func=AF.Identity, bias=0.0, scale=col(SCAL + 9)),
                    waits=(("tt", MMD), ("dlg", DBGD.get(("ff", tau - 2), 0))))
                prev_p8[("ec", bk)] = ECD
            fft = a16(FFT + ti * 1024, 1024)
            if "ffn" in sbuf_dbg:
                DBGD[("ff", tau)] = dma("gpsimd",
                    sbuf_dbg["ffn"][tau * 128:(tau + 1) * 128, :], fft,
                    waits=(("tc", ECD),))
            ffn_n = a16(FFN_N + ti * 1024, 1024)
            x1r = XR + (tau % 2) * 1024
            XLD = sdma("sync", "dxr", a32(x1r, 1024), x1_sp[tau * 128:(tau + 1) * 128, :],
                       waits=(("tv", prev_p8.get(("x1", tau % 2), P1V)),))
            FND = ln_norm(fft, ffn_n, extra_wait=(("tc", ECD),))
            ob_ = 2048 + (tau % 2) * 1024
            OV = vop(lambda e, ob_=ob_, x1r=x1r, ffn_n=ffn_n: e.tensor_tensor(
                out=a32(ob_, 1024), in0=a32(x1r, 1024), in1=ffn_n, op=OP.add),
                waits=(("tc", FND), ("dxr", XLD),
                       ("dlg", prev_p8.get(("od", tau % 2), 0))))
            prev_p8[("x1", tau % 2)] = OV
            for jh in range(2):
                prev_p8[("ev", grp + ti * 2 + jh)] = OV
            ODMA = dma("gpsimd", out_d[tau * 128:(tau + 1) * 128, :], a32(ob_, 1024),
                       waits=(("tv", OV),))
            prev_p8[("od", tau % 2)] = ODMA
        prev_p8[("mmb", bb % 2)] = MMD

    _dbg_copy(dbg_outs, dma, a16, a32, OB, C, F32)
    sch.op("gpsimd", None, waits=(("dlg", C["dlg"]),))
    return _finish(nc, sch, sems, stack)


def _dbg_copy(dbg_outs, dma, a16, a32, OB, C, F32):
    first = True
    ping = [0]
    for name, (dst, srct) in dbg_outs.items():
        rows, cols = dst.ap().shape
        isf32 = dst.ap().dtype == F32
        cmax = 2048 if isf32 else 8192
        for r0 in range(0, rows, 128):
            rr = min(128, rows - r0)
            for c0 in range(0, cols, cmax):
                cc = min(cmax, cols - c0)
                if isf32:
                    slot = (ping[0] % 2) * 2048
                    stg_full = a32(slot, cc)
                else:
                    slot = OB + (ping[0] % 2) * 8192
                    stg_full = a16(slot, cc)
                ping[0] += 1
                stg = stg_full if rr == 128 else stg_full[0:rr, :]
                w = [("dls", C["dls"]), ("dlg", C["dlg"])]
                if first:
                    w += [("tv", C["tv"]), ("tc", C["tc"])]
                    first = False
                LD = dma("sync", stg, srct[r0:r0 + rr, c0:c0 + cc], waits=tuple(w))
                dma("gpsimd", dst[r0:r0 + rr, c0:c0 + cc], stg, waits=(("dls", LD),))


def _finish(nc, sch, sems, stack):
    with nc.Block() as block:
        @block.sync
        def _(sync):
            sch.run("sync", sync, sems)

        @block.scalar
        def _(scalar):
            sch.run("scalar", scalar, sems)

        @block.vector
        def _(vector):
            sch.run("vector", vector, sems)

        @block.tensor
        def _(tensor):
            sch.run("tensor", tensor, sems)

        @block.gpsimd
        def _(gpsimd):
            sch.run("gpsimd", gpsimd, sems)

    stack.close()
    return nc


def make_masks():
    tri = (np.arange(CHK)[:, None] <= np.arange(CHK)[None, :]).astype(np.float32)
    mski = np.ones((128, CHK), np.float32)
    mski[64:128, :] = tri
    msk0 = np.zeros((128, CHK), np.float32)
    msk0[64:128, :] = tri
    return mski.astype(ml_dtypes.bfloat16), msk0.astype(ml_dtypes.bfloat16)


def kernel(L0_attn_hidden_states, L0_attn_Wq, L0_attn_Wk, L0_attn_Wv,
           L0_ffn_w_gate, L0_ffn_w_up, L0_ffn_w_down):
    x = np.ascontiguousarray(np.asarray(L0_attn_hidden_states, dtype=np.float32))
    B, S_, D_ = x.shape
    assert B == NCORES and D_ == DM
    mski, msk0 = make_masks()
    common = dict(
        wq=np.ascontiguousarray(np.asarray(L0_attn_Wq, np.float32)),
        wk=np.ascontiguousarray(np.asarray(L0_attn_Wk, np.float32)),
        wv=np.ascontiguousarray(np.asarray(L0_attn_Wv, np.float32)),
        wg=np.ascontiguousarray(np.asarray(L0_ffn_w_gate, np.float32)),
        wu=np.ascontiguousarray(np.asarray(L0_ffn_w_up, np.float32)),
        wd=np.ascontiguousarray(np.asarray(L0_ffn_w_down, np.float32)),
        mski=mski, msk0=msk0,
    )
    nc = build_nc(S=S_, ncores=NCORES)
    in_maps = [dict(common, x=np.ascontiguousarray(x[c])) for c in range(NCORES)]
    res = run_bass_kernel_spmd(nc, in_maps, core_ids=list(range(NCORES)))
    out = np.stack([np.asarray(res.results[c]["out"]) for c in range(NCORES)], axis=0)
    return out.astype(np.float32)
